# version 14
# speedup vs baseline: 1.5965x; 1.0011x over previous
import numpy as np
import ml_dtypes
import concourse.bass as bass
import concourse.mybir as mybir
from concourse.bass_utils import run_bass_kernel_spmd

F32 = mybir.dt.float32
BF16 = mybir.dt.bfloat16
I32 = mybir.dt.int32
AF = mybir.ActivationFunctionType
ALU = mybir.AluOpType
AX = mybir.AxisListType.X
NPBF16 = ml_dtypes.bfloat16

B, T, D, H, HD = 2, 8192, 1024, 16, 64
IF, DI, NST, DTR, DCONV = 2816, 2048, 16, 64, 4
TOK = 2048
EPS = 1e-5
QBLK = 16
KT = D // 128
NT = T // 128
NTO = TOK // 128
DIT = DI // 128
ITI = IF // 128
ATT_SCALE = 1.0 / 32.0


class Prog:
    ENGS = ("sync", "tensor", "vector", "scalar", "gpsimd")

    def __init__(self):
        self.nc = bass.Bass("TRN2", target_bir_lowering=False, debug=False, detect_race_conditions=False)
        self.plan = {e: [] for e in self.ENGS}
        self.cnt = {}
        self.mem = {}
        self._sems = {}
        self._ctxs = []
        self._lastwait = {e: {} for e in self.ENGS}

    def sem(self, name):
        if name not in self._sems:
            cm = self.nc.semaphore(name)
            self._sems[name] = (cm, cm.__enter__())
        return self._sems[name][1]

    def bump(self, name, k=1):
        self.sem(name)
        self.cnt[name] = self.cnt.get(name, 0) + k
        return self.cnt[name]

    def val(self, name):
        return self.cnt.get(name, 0)

    def step(self, eng, body, waits=(), inc=None, inc_amt=1):
        if eng == "vector" and inc == "ve":
            waits = list(waits) + [("ve", self.val("ve"))]
        elif eng == "scalar" and inc == "se":
            waits = list(waits) + [("se", self.val("se"))]
        lw = self._lastwait[eng]
        w = []
        for (s, v) in waits:
            if v > 0 and lw.get(s, 0) < v:
                w.append((s, v))
                lw[s] = max(lw.get(s, 0), v)
        if inc is not None:
            self.bump(inc, inc_amt)

        def fn(e, w=tuple(w), body=body, inc=inc, inc_amt=inc_amt):
            for (s, v) in w:
                e.wait_ge(self.sem(s), v)
            ins = body(e)
            if inc is not None:
                assert ins is not None
                ins.then_inc(self.sem(inc), inc_amt)
        self.plan[eng].append(fn)
        return self.val(inc) if inc is not None else None

    def dmas(self, eng, emitters, waits=(), sem="dx"):
        n = len(emitters)
        lw = self._lastwait[eng]
        w = []
        for (s, v) in waits:
            if v > 0 and lw.get(s, 0) < v:
                w.append((s, v))
                lw[s] = max(lw.get(s, 0), v)
        self.bump(sem, 16 * n)

        def fn(e, w=tuple(w), emitters=tuple(emitters), sem=sem):
            for (s, v) in w:
                e.wait_ge(self.sem(s), v)
            for em in emitters:
                em(e).then_inc(self.sem(sem), 16)
        self.plan[eng].append(fn)
        return self.val(sem)

    ARENA_BYTES = 211200

    def _arena(self):
        if "_arena_t" not in self.mem:
            cm = self.nc.sbuf_tensor("arena", [128, self.ARENA_BYTES], mybir.dt.uint8)
            h = cm.__enter__()
            self._ctxs.append(cm)
            self.mem["_arena_t"] = h
            self.mem["_arena_off"] = 0
            self.mem["_arena_allocs"] = []
        return self.mem["_arena_t"]

    def sbuf(self, name, shape, dt):
        t = self._arena()
        esz = {F32: 4, BF16: 2, I32: 4, mybir.dt.uint8: 1, mybir.dt.uint32: 4}[dt]
        P = shape[0]
        inner = 1
        for s in shape[1:]:
            inner *= s
        nbytes = inner * esz
        off = self.mem["_arena_off"]
        off = (off + 511) // 512 * 512
        assert off + nbytes <= self.ARENA_BYTES * 1, \
            f"arena overflow at {name}: {off}+{nbytes} > {self.ARENA_BYTES}"
        ap = t[0:P, off:off + nbytes].bitcast(dt)
        if len(shape) == 2:
            ap = ap
        elif len(shape) == 3:
            ap = ap.rearrange("p (a b) -> p a b", a=shape[1])
        elif len(shape) == 4:
            ap = ap.rearrange("p (a b c) -> p a b c", a=shape[1], b=shape[2])
        else:
            raise ValueError(shape)
        self.mem["_arena_off"] = off + nbytes
        self.mem["_arena_allocs"].append((name, off, off + nbytes))
        return ap

    def release(self, *names):
        allocs = self.mem["_arena_allocs"]
        want = set(names)
        got = set()
        while allocs and allocs[-1][0] in want:
            n, s, e = allocs.pop()
            got.add(n)
        assert got == want, f"release mismatch: want {want}, popped {got} " \
            f"(top of stack: {[a[0] for a in allocs[-6:]]})"
        self.mem["_arena_off"] = allocs[-1][2] if allocs else 0

    def barrier(self):
        snap = [(s, self.val(s)) for s in list(self.cnt) if not s.startswith("_")]
        for eng in self.ENGS:
            self.step(eng, lambda e: None, waits=snap)

    def psum(self, name, shape, dt=F32):
        cm = self.nc.psum_tensor(name, shape, dt)
        h = cm.__enter__()
        self._ctxs.append(cm)
        self.mem.setdefault("_psum_cms", {})[name] = cm
        return h

    def psum_release(self, *names):
        cms = self.mem.get("_psum_cms", {})
        order = list(cms)
        for n in sorted(names, key=lambda n: -order.index(n)):
            cm = cms.pop(n)
            cm.__exit__(None, None, None)
            self._ctxs.remove(cm)

    def emit(self):
        with self.nc.Block() as block:
            for eng_name, deco in (("sync", block.sync), ("tensor", block.tensor),
                                   ("vector", block.vector), ("scalar", block.scalar),
                                   ("gpsimd", block.gpsimd)):
                fns = self.plan[eng_name]

                def run(e, fns=fns):
                    for f in fns:
                        f(e)
                deco(run)
        for cm in reversed(self._ctxs):
            cm.__exit__(None, None, None)
        for name in list(self._sems):
            self._sems[name][0].__exit__(None, None, None)
        return self.nc


def emit_quant(p, src, dst, scr, scrP, scrQ, scrR, m_t, r_t, s_t, F, waits, inc="ve"):
    nb = F // QBLK
    src3 = src.rearrange("p (nb q) -> p nb q", q=QBLK)
    rb = r_t[:, :nb].unsqueeze(2).broadcast_to([128, nb, QBLK])
    sb2 = s_t[:, :nb].unsqueeze(2).broadcast_to([128, nb, QBLK])
    dst3 = dst.rearrange("p (nb q) -> p nb q", q=QBLK)
    scr3 = scr[:, :F].rearrange("p (nb q) -> p nb q", q=QBLK)
    scrQ3 = scrQ[:, :F].rearrange("p (nb q) -> p nb q", q=QBLK)
    A = scrP[:, :F]
    QA = scrQ[:, :F]
    QB = scr[:, :F]
    SG = scrR[:, :F]

    def body0(v):
        return v.reduce_max(m_t[:, :nb], src3, axis=AX, apply_absolute_value=True)

    def body1(v):
        return v.reciprocal(r_t[:, :nb], m_t[:, :nb])

    def body2(v):
        v.tensor_scalar(r_t[:, :nb], r_t[:, :nb], 6.0, 1.0e12, ALU.mult, ALU.min)
        return v.tensor_scalar(s_t[:, :nb], m_t[:, :nb], 1.0 / 6.0, 1e-12, ALU.mult, ALU.max)

    def body(v):
        v.tensor_tensor(scr3, src3, rb, ALU.mult)
        v.scalar_tensor_tensor(A, scr[:, :F], -1.0, scr[:, :F], ALU.mult, ALU.max)
        v.tensor_single_scalar(SG, scr[:, :F], 0.0, ALU.is_gt)
        v.tensor_scalar(SG, SG, 2.0, -1.0, ALU.mult, ALU.add)
        v.tensor_single_scalar(QA, A, 0.25, ALU.is_gt)
        v.scalar_tensor_tensor(QA, A, 0.75, QA, ALU.is_gt, ALU.add)
        v.scalar_tensor_tensor(QA, A, 1.25, QA, ALU.is_gt, ALU.add)
        v.scalar_tensor_tensor(QA, A, 1.75, QA, ALU.is_gt, ALU.add)
        v.tensor_single_scalar(QB, A, 2.5, ALU.is_gt)
        v.scalar_tensor_tensor(QB, A, 3.5, QB, ALU.is_gt, ALU.add)
        v.scalar_tensor_tensor(QB, A, 5.0, QB, ALU.is_gt, ALU.add)
        v.scalar_tensor_tensor(QB, A, 5.0, QB, ALU.is_gt, ALU.add)
        v.scalar_tensor_tensor(QA, QA, 0.5, QB, ALU.mult, ALU.add)
        v.tensor_tensor(QA, QA, SG, ALU.mult)
        return v.tensor_tensor(dst3, scrQ3, sb2, ALU.mult)
    p.step("vector", body0, waits=waits, inc=inc)
    p.step("vector", body1, waits=(), inc=inc)
    p.step("vector", body2, waits=(), inc=inc)
    return p.step("vector", body, waits=(), inc=inc)


def build_L1():
    p = Prog()
    nc = p.nc

    x0_rot = nc.dram_tensor("x0_rot", [T, D], F32, kind="ExternalInput").ap()
    x0T_own = nc.dram_tensor("x0T_own", [D, TOK], F32, kind="ExternalInput").ap()
    wq_t = nc.dram_tensor("wq_t", [D, D], F32, kind="ExternalInput").ap()
    wk_t = nc.dram_tensor("wk_t", [D, D], F32, kind="ExternalInput").ap()
    wv_t = nc.dram_tensor("wv_t", [D, D], F32, kind="ExternalInput").ap()
    wu_t = nc.dram_tensor("wu_t", [D, D], F32, kind="ExternalInput").ap()
    bu_in = nc.dram_tensor("bu_in", [128, KT], F32, kind="ExternalInput").ap()
    w_gate = nc.dram_tensor("w_gate", [IF, D], F32, kind="ExternalInput").ap()
    w_up = nc.dram_tensor("w_up", [IF, D], F32, kind="ExternalInput").ap()
    w_down = nc.dram_tensor("w_down", [D, IF], F32, kind="ExternalInput").ap()
    identb = nc.dram_tensor("identb", [128, 128], BF16, kind="ExternalInput").ap()

    x2T_out = nc.dram_tensor("x2T_out", [D, TOK], F32, kind="ExternalOutput").ap()
    x1T_dbg = nc.dram_tensor("x1T_dbg", [D, TOK], F32, kind="ExternalOutput").ap()
    xqT_dbg = nc.dram_tensor("xqT_dbg", [D, TOK], BF16, kind="ExternalOutput").ap()
    yhp_dbg = nc.dram_tensor("yhp_dbg", [128, TOK], F32, kind="ExternalOutput").ap()
    attT_dbg = nc.dram_tensor("attT_dbg", [128, 2048], BF16, kind="ExternalOutput").ap()
    kv_dbg = nc.dram_tensor("kv_dbg", [128, 2048], BF16, kind="ExternalOutput").ap()
    ktd_dbg = nc.dram_tensor("ktd_dbg", [D, T], BF16, kind="ExternalOutput").ap()
    vd_dbg = nc.dram_tensor("vd_dbg", [T, H * 65], BF16, kind="ExternalOutput").ap()
    rowt_dbg = nc.dram_tensor("rowt_dbg", [64, TOK], F32, kind="ExternalOutput").ap()
    gate_qt_out = nc.dram_tensor("gate_qt_out", [D, IF], BF16, kind="ExternalOutput").ap()
    up_qt_out = nc.dram_tensor("up_qt_out", [D, IF], BF16, kind="ExternalOutput").ap()
    down_qt_out = nc.dram_tensor("down_qt_out", [IF, D], BF16, kind="ExternalOutput").ap()

    kT_d = nc.dram_tensor("kT_d", [D, T], BF16).ap()
    v_d = nc.dram_tensor("v_d", [T, H * 65], BF16).ap()
    gq_d = nc.dram_tensor("gq_d", [D, IF], BF16).ap()
    uq_d = nc.dram_tensor("uq_d", [D, IF], BF16).ap()
    dq_d = nc.dram_tensor("dq_d", [IF, D], BF16).ap()

    identb_s = p.sbuf("identb_s", [128, 128], BF16)
    identf_s = p.sbuf("identf_s", [128, 128], F32)
    ones32 = p.sbuf("ones32", [128, 32], BF16)
    patt = p.sbuf("patt", [64, 128], F32)
    bu_s = p.sbuf("bu_s", [128, KT], F32)
    cst = p.sbuf("cst", [128, 2], F32)
    stat = p.sbuf("stat", [128, 8], F32)
    sqs = p.sbuf("sqs", [128, D], F32)
    x1T = p.sbuf("x1T", [128, KT, TOK], F32)
    qT_s = p.sbuf("qT_s", [128, KT, TOK], BF16)

    wq_s = p.sbuf("wq_s", [128, KT, D], BF16)
    wk_s = p.sbuf("wk_s", [128, KT, D], BF16)
    wv_s = p.sbuf("wv_s", [128, KT, D], BF16)
    wu_s = p.sbuf("wu_s", [128, KT, D], BF16)
    x0t = p.sbuf("x0t", [128, 2, D], F32)
    xnf = p.sbuf("xnf", [128, 2, D], F32)
    xnT = p.sbuf("xnT", [128, KT, 128], BF16)
    kvev = p.sbuf("kvev", [128, 2, 512], BF16)
    vtile = p.sbuf("vtile", [128, 2, H * 65], BF16)

    ps_y0 = p.psum("ps_y0", [128, 512])
    ps_y1 = p.psum("ps_y1", [128, 512])
    ps_row0 = p.psum("ps_row0", [64, 512])
    ps_row1 = p.psum("ps_row1", [64, 512])
    pb = [p.psum(f"pb{i}", [128, 512]) for i in range(4)]

    DX_W = p.dmas("gpsimd", [
        lambda g: g.dma_start(out=wq_s[:, :, :], in_=wq_t.rearrange("(kt p) o -> p kt o", p=128)),
        lambda g: g.dma_start(out=wk_s[:, :, :], in_=wk_t.rearrange("(kt p) o -> p kt o", p=128)),
        lambda g: g.dma_start(out=wv_s[:, :, :], in_=wv_t.rearrange("(kt p) o -> p kt o", p=128)),
        lambda g: g.dma_start(out=wu_s[:, :, :], in_=wu_t.rearrange("(kt p) o -> p kt o", p=128)),
        lambda g: g.dma_start(out=bu_s[:, :], in_=bu_in),
    ])
    DW_ID = p.dmas("sync", [
        lambda sy: sy.dma_start(out=identb_s[:, :], in_=identb),
    ], sem="dw")

    def b(v):
        v.memset(ones32[:, :], 0.0)
        v.memset(ones32[:, 0:1], 1.0)
        v.memset(patt[:, :], 0.0)
        v.memset(patt[0:1, 0:64], 1.0)
        v.memset(patt[32:33, 64:128], 1.0)
        v.memset(cst[:, 0:1], EPS)
        return v.tensor_copy(identf_s[:, :], identb_s[:, :])
    p.step("vector", b, waits=[("dw", DW_ID)], inc="ve")

    norm_done = {}
    ssq_done = {}
    trev_done = {}
    kvev_dma = {}
    vdma_done = {}
    kv_pe = {}
    qev_pe = {}

    for i in range(NT):
        sl = i % 2
        w = [("ve", norm_done.get(i - 2, 0))]
        dx_in = p.dmas("gpsimd", [
            lambda g, i=i, sl=sl: g.dma_start(out=x0t[:, sl, :], in_=x0_rot[i * 128:(i + 1) * 128, :]),
        ], waits=w)

        vr = p.step("vector",
                    lambda v, sl=sl: v.reduce_sum(stat[:, 4 * sl:4 * sl + 1], x0t[:, sl, :], axis=AX),
                    waits=[("dx", dx_in)], inc="ve")
        p.step("vector",
               lambda v, sl=sl: v.tensor_single_scalar(stat[:, 4 * sl:4 * sl + 1],
                                                       stat[:, 4 * sl:4 * sl + 1], 1.0 / D, ALU.mult),
               waits=[("ve", vr)], inc="ve")

        w = [("dx", dx_in), ("ve", ssq_done.get(i - 1, 0))]
        se_sq = p.step("scalar",
                       lambda s, sl=sl: s.activation(out=sqs[:, :], in_=x0t[:, sl, :], func=AF.Square),
                       waits=w, inc="se")

        vr = p.step("vector",
                    lambda v, sl=sl: v.reduce_sum(stat[:, 4 * sl + 1:4 * sl + 2], sqs[:, :], axis=AX),
                    waits=[("se", se_sq)], inc="ve")
        def b(v, sl=sl):
            v.tensor_single_scalar(stat[:, 4 * sl + 1:4 * sl + 2],
                                   stat[:, 4 * sl + 1:4 * sl + 2], 1.0 / D, ALU.mult)
            return v.tensor_single_scalar(stat[:, 4 * sl + 2:4 * sl + 3], stat[:, 4 * sl:4 * sl + 1],
                                          stat[:, 4 * sl:4 * sl + 1], ALU.mult)
        p.step("vector", b, waits=[("ve", vr)], inc="ve")
        ssq_done[i] = p.step("vector",
                             lambda v, sl=sl: v.tensor_tensor(stat[:, 4 * sl + 1:4 * sl + 2],
                                                              stat[:, 4 * sl + 1:4 * sl + 2],
                                                              stat[:, 4 * sl + 2:4 * sl + 3],
                                                              ALU.subtract),
                             waits=(), inc="ve")

        se_std = p.step("scalar",
                        lambda s, sl=sl: s.activation(out=stat[:, 4 * sl + 3:4 * sl + 4],
                                                      in_=stat[:, 4 * sl + 1:4 * sl + 2],
                                                      func=AF.Sqrt, bias=cst[:, 0:1]),
                        waits=[("ve", ssq_done[i])], inc="se")

        p.step("vector",
               lambda v, sl=sl: v.reciprocal(stat[:, 4 * sl + 2:4 * sl + 3],
                                             stat[:, 4 * sl + 3:4 * sl + 4]),
               waits=[("se", se_std)], inc="ve")
        norm_done[i] = p.step("vector",
                              lambda v, sl=sl: v.tensor_scalar(xnf[:, sl, :], x0t[:, sl, :],
                                                               stat[:, 4 * sl:4 * sl + 1],
                                                               stat[:, 4 * sl + 2:4 * sl + 3],
                                                               ALU.subtract, ALU.mult),
                              waits=(), inc="ve")

        for half in range(2):
            w = [("ve", norm_done[i]), ("dw", DW_ID), ("se", p.val("se"))]
            if half == 0 and i >= 1:
                w.append(("pe", kv_pe[i - 1]))
            def b(tn, sl=sl, half=half):
                last = None
                for j in range(4):
                    jj = half * 4 + j
                    last = tn.transpose(pb[j][:, 0:128],
                                        xnf[:, sl, jj * 128:(jj + 1) * 128], identf_s[:, :])
                return last
            pe_tr = p.step("tensor", b, waits=w, inc="pe")
            def b(s, half=half):
                last = None
                for j in range(4):
                    last = s.copy(out=xnT[:, half * 4 + j, :], in_=pb[j][:, 0:128])
                return last
            p.step("scalar", b, waits=[("pe", pe_tr)], inc="se")
        trev_done[i] = p.val("se")

        for og in range(2):
            w = [("se", trev_done[i]), ("dx", DX_W), ("se", p.val("se"))]
            def b(tn, og=og):
                last = None
                for ot in range(4):
                    o = og * 4 + ot
                    for kt in range(KT):
                        last = tn.matmul(pb[ot][:, 0:128],
                                         wk_s[:, kt, o * 128:(o + 1) * 128],
                                         xnT[:, kt, :], start=(kt == 0), stop=(kt == KT - 1))
                return last
            pe_k = p.step("tensor", b, waits=w, inc="pe")
            w2 = [("pe", pe_k), ("dx", kvev_dma.get((i - 1, og), 0))]
            def b(s, og=og):
                last = None
                for ot in range(4):
                    last = s.copy(out=kvev[:, og, ot * 128:(ot + 1) * 128], in_=pb[ot][:, 0:128])
                return last
            se_ev = p.step("scalar", b, waits=w2, inc="se")
            kvev_dma[(i, og)] = p.dmas("gpsimd", [
                lambda g, og=og, i=i: g.dma_start(
                    out=kT_d.rearrange("(g p) t -> p g t", p=128)[:, og * 4:og * 4 + 4,
                                                                 i * 128:(i + 1) * 128],
                    in_=kvev[:, og, :].rearrange("p (a b) -> p a b", a=4)),
            ], waits=[("se", se_ev)])

        for oc in range(2):
            def b(tn, oc=oc):
                last = None
                for kt in range(KT):
                    last = tn.matmul(pb[oc][:, :], xnT[:, kt, :],
                                     wv_s[:, kt, oc * 512:(oc + 1) * 512],
                                     start=(kt == 0), stop=(kt == KT - 1))
                return last
            pe_v = p.step("tensor", b, waits=[("se", trev_done[i]), ("se", p.val("se"))], inc="pe")
            w2 = [("pe", pe_v)]
            if oc == 0:
                w2.append(("dx", vdma_done.get(i - 2, 0)))
            p.step("scalar",
                   lambda s, oc=oc, sl=sl: s.copy(
                       out=vtile[:, sl, :].rearrange("p (h c) -> p h c", c=65)[:, oc * 8:oc * 8 + 8, 0:64],
                       in_=pb[oc][:, :].rearrange("p (h c) -> p h c", c=64)),
                   waits=w2, inc="se")
        se_vev = p.val("se")
        ve_ones = p.step("vector",
                         lambda v, sl=sl: v.memset(
                             vtile[:, sl, :].rearrange("p (h c) -> p h c", c=65)[:, :, 64:65], 1.0),
                         waits=[("se", se_vev)], inc="ve")
        vdma_done[i] = p.dmas("gpsimd", [
            lambda g, i=i, sl=sl: g.dma_start(out=v_d[i * 128:(i + 1) * 128, :], in_=vtile[:, sl, :]),
        ], waits=[("ve", ve_ones)])

        if i < NTO:
            for og in range(2):
                def b(tn, og=og):
                    last = None
                    for ot in range(4):
                        o = og * 4 + ot
                        for kt in range(KT):
                            last = tn.matmul(pb[ot][:, 0:128],
                                             wq_s[:, kt, o * 128:(o + 1) * 128],
                                             xnT[:, kt, :], start=(kt == 0), stop=(kt == KT - 1))
                    return last
                pe_q = p.step("tensor", b, waits=[("se", trev_done[i]),
                                                  ("se", p.val("se"))], inc="pe")
                def b(s, og=og, i=i):
                    last = None
                    for ot in range(4):
                        last = s.copy(out=qT_s[:, og * 4 + ot, i * 128:(i + 1) * 128],
                                      in_=pb[ot][:, 0:128])
                    return last
                p.step("scalar", b, waits=[("pe", pe_q)], inc="se")
        kv_pe[i] = p.val("pe")

    KV_ALL_DX = p.val("dx")
    KV_ALL_SE = p.val("se")
    p.barrier()
    p.release("vtile", "kvev", "xnT", "xnf", "x0t", "wu_s", "wv_s", "wk_s", "wq_s")

    ynb = p.sbuf("ynb", [128, KT, TOK], BF16)
    kTp = p.sbuf("kTp", [128, 1, T], BF16)
    vp = p.sbuf("vp", [128, 1, 64, 130], BF16)
    attT = p.sbuf("attT", [128, 2, 2, 512], BF16)
    yhp = p.sbuf("yhp", [128, TOK], F32)
    rowt = p.sbuf("rowt", [64, TOK], F32)

    ve_rowinit = p.step("vector", lambda v: v.memset(rowt[0:64, :], 1.0), waits=[], inc="ve")

    p.psum_release("pb0", "pb1", "pb2", "pb3")
    scb = [p.psum(f"scb{i}", [128, 1024]) for i in range(2)]
    hp_dma = {}
    hp_lastpe = {}
    scb_war_ve = 0
    ev_prev = 0
    for hp in range(H // 2):
        sl = 0
        w = [("dx", KV_ALL_DX), ("se", KV_ALL_SE)]
        if hp >= 1:
            w.append(("pe", hp_lastpe[hp - 1]))
        hp_dma[hp] = p.dmas("gpsimd", [
            lambda g, hp=hp, sl=sl: g.dma_start(out=kTp[:, sl, :], in_=kT_d[hp * 128:(hp + 1) * 128, :]),
            lambda g, hp=hp, sl=sl: g.dma_start(
                out=vp[:, sl, :, :],
                in_=v_d.rearrange("(sb p) c -> p sb c", p=128)[:, :, hp * 130:(hp + 1) * 130]),
        ], waits=w, sem="dk")

        for tc in range(4):
            exp_hist = {}
            av_hist = {}
            sc_hist = {}

            def emit_sc(k, hp=hp, tc=tc, sl=sl):
                dsl = k % 2
                slot_free = exp_hist.get(k - 2, p.val("se") if k < 2 else 0)
                w = [("dk", hp_dma[hp]), ("ve", scb_war_ve),
                     ("se", slot_free)]
                def b(tn, sl=sl, hp=hp, tc=tc, sb=k, dsl=dsl):
                    tn.matmul(scb[dsl][:, 0:512], kTp[0:64, sl, sb * 128:(sb + 1) * 128],
                              qT_s[0:64, hp, tc * 512:(tc + 1) * 512],
                              start=True, stop=True, tile_position=(0, 0))
                    return tn.matmul(scb[dsl][:, 512:1024], kTp[64:128, sl, sb * 128:(sb + 1) * 128],
                                     qT_s[64:128, hp, tc * 512:(tc + 1) * 512],
                                     start=True, stop=True, tile_position=(64, 0))
                return p.step("tensor", b, waits=w, inc="pe")

            sc_hist[0] = emit_sc(0)
            sc_hist[1] = emit_sc(1)
            for sb in range(64):
                dsl = sb % 2
                w = [("pe", sc_hist[sb]), ("pe", av_hist.get(sb - 2, 0))]
                se_exp = p.step("scalar",
                                lambda s, dsl=dsl: s.activation(
                                    out=attT[:, dsl, :, :].rearrange("p a b -> p (a b)"),
                                    in_=scb[dsl][:, :], func=AF.Exp, scale=ATT_SCALE),
                                waits=w, inc="se")
                exp_hist[sb] = se_exp

                w = [("se", se_exp)]
                if sb == 0:
                    w.append(("ve", ev_prev))
                def b(tn, sl=sl, sb=sb, dsl=dsl):
                    st0 = (sb == 0)
                    sp = (sb == 63)
                    tn.matmul(ps_y0[0:64, :], vp[:, sl, sb, 0:64], attT[:, dsl, 0, :],
                              start=st0, stop=sp, tile_position=(0, 0))
                    tn.matmul(ps_y1[64:128, :], vp[:, sl, sb, 65:129], attT[:, dsl, 1, :],
                              start=st0, stop=sp, tile_position=(0, 64))
                    tn.matmul(ps_row0[0:32, :], ones32[:, :], attT[:, dsl, 0, :],
                              start=st0, stop=sp, tile_position=(0, 0))
                    return tn.matmul(ps_row1[32:64, :], ones32[:, :], attT[:, dsl, 1, :],
                                     start=st0, stop=sp, tile_position=(0, 32))
                av_hist[sb] = p.step("tensor", b, waits=w, inc="pe")
                if sb + 2 < 64:
                    sc_hist[sb + 2] = emit_sc(sb + 2)

            def b(v, tc=tc):
                v.tensor_copy(yhp[0:64, tc * 512:(tc + 1) * 512], ps_y0[0:64, :])
                v.tensor_copy(yhp[64:128, tc * 512:(tc + 1) * 512], ps_y1[64:128, :])
                v.tensor_copy(rowt[0:1, tc * 512:(tc + 1) * 512], ps_row0[0:1, :])
                return v.tensor_copy(rowt[32:33, tc * 512:(tc + 1) * 512], ps_row1[32:33, :])
            ev_prev = p.step("vector", b, waits=[("pe", p.val("pe"))], inc="ve")

        ve_rec = p.step("vector", lambda v: v.reciprocal(rowt[0:64, :], rowt[0:64, :]),
                        waits=[("ve", ev_prev)], inc="ve")

        w = [("ve", ve_rec), ("se", p.val("se"))]
        def b(tn):
            last = None
            for tc in range(4):
                last = tn.matmul(scb[tc // 2][:, (tc % 2) * 512:(tc % 2) * 512 + 512],
                                 patt[:, :], rowt[0:64, tc * 512:(tc + 1) * 512],
                                 start=True, stop=True)
            return last
        pe_bc = p.step("tensor", b, waits=w, inc="pe")

        def b(v, hp=hp):
            last = None
            for tc in range(4):
                last = v.tensor_tensor(ynb[:, hp, tc * 512:(tc + 1) * 512],
                                       yhp[:, tc * 512:(tc + 1) * 512],
                                       scb[tc // 2][:, (tc % 2) * 512:(tc % 2) * 512 + 512],
                                       ALU.mult)
            return last
        scb_war_ve = p.step("vector", b, waits=[("pe", pe_bc)], inc="ve")
        hp_lastpe[hp] = p.val("pe")
    p.psum_release("scb1", "scb0")
    pb = [p.psum(f"pbo{i}", [128, 512]) for i in range(4)]

    ATT_YNB_VE = p.val("ve")
    p.dmas("gpsimd", [
        lambda g: g.dma_start(out=yhp_dbg, in_=yhp[:, :]),
        lambda g: g.dma_start(out=rowt_dbg, in_=rowt[0:64, :]),
        lambda g: g.dma_start(out=attT_dbg, in_=attT[:, :, :, :].rearrange("p a b c -> p (a b c)")),
        lambda g: g.dma_start(out=kv_dbg, in_=kTp[:, 0, 0:2048]),
        lambda g: g.dma_start(out=ktd_dbg, in_=kT_d),
        lambda g: g.dma_start(out=vd_dbg, in_=v_d),
    ], waits=[("ve", ATT_YNB_VE)], sem="out")
    p.barrier()
    p.release("rowt", "yhp", "attT", "vp", "kTp")
    wu2_s = p.sbuf("wu2_s", [128, KT, D], BF16)

    dx_x0T = p.dmas("gpsimd", [
        lambda g: g.dma_start(out=x1T[:, :, :], in_=x0T_own.rearrange("(kt p) t -> p kt t", p=128)),
        lambda g: g.dma_start(out=wu2_s[:, :, :], in_=wu_t.rearrange("(kt p) o -> p kt o", p=128)),
    ])
    dn_free = {}
    for ot in range(KT):
        for tc in range(4):
            pbi = (ot * 4 + tc) % 4
            w = [("ve", ATT_YNB_VE), ("ve", dn_free.get(pbi, 0)), ("dx", dx_x0T)]
            def b(tn, ot=ot, tc=tc, pbi=pbi):
                last = None
                for hp in range(KT):
                    last = tn.matmul(pb[pbi][:, :], wu2_s[:, hp, ot * 128:(ot + 1) * 128],
                                     ynb[:, hp, tc * 512:(tc + 1) * 512],
                                     start=(hp == 0), stop=(hp == KT - 1))
                return last
            pe_op = p.step("tensor", b, waits=w, inc="pe")
            dn_free[pbi] = p.step(
                "vector",
                lambda v, ot=ot, tc=tc, pbi=pbi: v.scalar_tensor_tensor(
                    x1T[:, ot, tc * 512:(tc + 1) * 512], pb[pbi][:, :], bu_s[:, ot:ot + 1],
                    x1T[:, ot, tc * 512:(tc + 1) * 512], ALU.add, ALU.add),
                waits=[("pe", pe_op), ("dx", dx_x0T)], inc="ve")
    X1_VE = p.val("ve")
    p.dmas("gpsimd", [
        lambda g: g.dma_start(out=x1T_dbg.rearrange("(kt p) t -> p kt t", p=128), in_=x1T[:, :, :]),
    ], waits=[("ve", X1_VE)], sem="out")
    p.barrier()
    p.release("wu2_s", "ynb", "qT_s")
    qscr = p.sbuf("qscr", [128, IF], F32)
    qscrP = p.sbuf("qscrP", [128, IF], F32)
    qscrQ = p.sbuf("qscrQ", [128, IF], F32)
    qscrR = p.sbuf("qscrR", [128, IF], F32)
    qm = p.sbuf("qm", [128, IF // QBLK], F32)
    qr = p.sbuf("qr", [128, IF // QBLK], F32)
    qs = p.sbuf("qs", [128, IF // QBLK], F32)
    wfq = p.sbuf("wfq", [128, 2, IF], F32)
    wfqo = p.sbuf("wfqo", [128, 2, IF], F32)
    kvev = p.sbuf("kvev2", [128, 2, 512], BF16)

    def quant_transpose_weight(src_dram, dst_dram, nrow_tiles, Fw, tag):
        ncol_tiles = Fw // 128
        qdone = {}
        trpe = {}
        evdma = {}
        for gI in range(nrow_tiles):
            sl = gI % 2
            w = [("ve", qdone.get(gI - 2, 0))]
            dx_in = p.dmas("gpsimd", [
                lambda g, gI=gI, sl=sl: g.dma_start(out=wfq[:, sl, 0:Fw],
                                                    in_=src_dram[gI * 128:(gI + 1) * 128, :]),
            ], waits=w)
            qw = [("dx", dx_in), ("pe", trpe.get(gI - 2, 0))]
            qdone[gI] = emit_quant(p, wfq[:, sl, 0:Fw], wfqo[:, sl, 0:Fw],
                                   qscr, qscrP, qscrQ, qscrR, qm, qr, qs, Fw, waits=qw)
            for half in range(0, ncol_tiles, 4):
                nh = min(4, ncol_tiles - half)
                hsl = (half // 4) % 2
                w = [("ve", qdone[gI]), ("se", p.val("se"))]
                def b(tn, sl=sl, half=half, nh=nh):
                    last = None
                    for j in range(nh):
                        jj = half + j
                        last = tn.transpose(pb[j][:, 0:128],
                                            wfqo[:, sl, jj * 128:(jj + 1) * 128], identf_s[:, :])
                    return last
                pe_tr = p.step("tensor", b, waits=w, inc="pe")
                trpe[gI] = pe_tr
                w2 = [("pe", pe_tr), ("dx", evdma.get(hsl, 0))]
                def b(s, nh=nh, hsl=hsl):
                    last = None
                    for j in range(nh):
                        last = s.copy(out=kvev[:, hsl, j * 128:(j + 1) * 128], in_=pb[j][:, 0:128])
                    return last
                se_ev = p.step("scalar", b, waits=w2, inc="se")
                evdma[hsl] = p.dmas("gpsimd", [
                    lambda g, gI=gI, half=half, nh=nh, hsl=hsl: g.dma_start(
                        out=dst_dram.rearrange("(a p) r -> p a r", p=128)
                        [:, half:half + nh, gI * 128:(gI + 1) * 128],
                        in_=kvev[:, hsl, 0:nh * 128].rearrange("p (a b) -> p a b", b=128)),
                ], waits=[("se", se_ev)])

    quant_transpose_weight(w_gate, gq_d, ITI, D, "g")
    quant_transpose_weight(w_up, uq_d, ITI, D, "u")
    quant_transpose_weight(w_down, dq_d, KT, IF, "d")
    FQ_DX = p.val("dx")
    p.barrier()
    p.release("kvev2", "wfqo", "wfq", "qs", "qr", "qm", "qscrR", "qscrQ", "qscrP", "qscr")
    qscr = p.sbuf("qscrF", [128, D], F32)
    qscrP = p.sbuf("qscrPF", [128, D], F32)
    qscrQ = p.sbuf("qscrQF", [128, D], F32)
    qscrR = p.sbuf("qscrRF", [128, D], F32)
    qm = p.sbuf("qmF", [128, D // QBLK], F32)
    qr = p.sbuf("qrF", [128, D // QBLK], F32)
    qs = p.sbuf("qsF", [128, D // QBLK], F32)
    xtm = p.sbuf("xtm", [128, 2, D], F32)
    xq = p.sbuf("xq", [128, 2, D], F32)
    xqT = p.sbuf("xqT", [128, KT, TOK], BF16)
    gch = p.sbuf("gch", [128, KT, 512], BF16)
    uch = p.sbuf("uch", [128, KT, 512], BF16)
    dch = p.sbuf("dch", [128, 4, D], BF16)
    gsc = p.sbuf("gsc", [128, 512], F32)
    hsc = p.sbuf("hsc", [128, 512], F32)
    hq = p.sbuf("hq", [128, 512], F32)
    hqT = p.sbuf("hqT", [128, 4, TOK], BF16)

    emit_ffn(p, dict(
        xT=x1T, xtm=xtm, xq=xq, xqT=xqT, gch=gch, uch=uch, dch=dch,
        gsc=gsc, hsc=hsc, hq=hq, hqT=hqT, stat=stat, sqs=sqs,
        qscr=qscr, qscrP=qscrP, qscrQ=qscrQ, qscrR=qscrR, qm=qm, qr=qr, qs=qs, cst=cst,
        identb_s=identb_s, identf_s=identf_s,
        pb=pb, gq_d=gq_d, uq_d=uq_d, dq_d=dq_d, xqT_dbg=xqT_dbg,
        ready_ve=X1_VE, ready_dx=FQ_DX))

    OUT = p.dmas("gpsimd", [
        lambda g: g.dma_start(out=x2T_out.rearrange("(kt p) t -> p kt t", p=128), in_=x1T[:, :, :]),
        lambda g: g.dma_start(out=gate_qt_out, in_=gq_d),
        lambda g: g.dma_start(out=up_qt_out, in_=uq_d),
        lambda g: g.dma_start(out=down_qt_out, in_=dq_d),
    ], waits=[("ve", p.val("ve")), ("se", p.val("se")), ("pe", p.val("pe")),
              ("dx", p.val("dx")), ("dc", p.val("dc"))], sem="out")
    p.step("gpsimd", lambda g: None, waits=[("out", OUT)])

    return p


def emit_ffn(p, t):
    xT, xtm, xq, xqT = t["xT"], t["xtm"], t["xq"], t["xqT"]
    gch, uch, dch = t["gch"], t["uch"], t["dch"]
    gsc, hsc, hq, hqT = t["gsc"], t["hsc"], t["hq"], t["hqT"]
    stat, sqs = t["stat"], t["sqs"]
    cst = t["cst"]
    identb_s, identf_s = t["identb_s"], t["identf_s"]
    pb = t["pb"]
    gq_d, uq_d, dq_d = t["gq_d"], t["uq_d"], t["dq_d"]
    ready_ve = t["ready_ve"]
    ready_dx = t["ready_dx"]
    ntiles = t.get("ntiles", NTO)
    tcs = t.get("tcs", [(i * 512, 512) for i in range(4)])

    ln_done = {}
    xq_trpe = {}
    for tt in range(ntiles):
        sl = tt % 2
        for half in range(2):
            w = [("ve", ready_ve), ("se", p.val("se"))]
            if half == 0:
                w.append(("ve", ln_done.get(tt - 2, 0)))
            def b(tn, tt=tt, half=half):
                last = None
                for j in range(4):
                    jj = half * 4 + j
                    last = tn.transpose(pb[j][:, 0:128],
                                        xT[:, jj, tt * 128:(tt + 1) * 128], identf_s[:, :])
                return last
            pe_tr = p.step("tensor", b, waits=w, inc="pe")
            def b(s, sl=sl, half=half):
                last = None
                for j in range(4):
                    last = s.copy(out=xtm[:, sl, half * 512 + j * 128:half * 512 + (j + 1) * 128],
                                  in_=pb[j][:, 0:128])
                return last
            p.step("scalar", b, waits=[("pe", pe_tr)], inc="se")
        se_tr = p.val("se")

        vr = p.step("vector",
                    lambda v, sl=sl: v.reduce_sum(stat[:, 0:1], xtm[:, sl, :], axis=AX),
                    waits=[("se", se_tr)], inc="ve")
        p.step("vector",
               lambda v: v.tensor_single_scalar(stat[:, 0:1], stat[:, 0:1], 1.0 / D, ALU.mult),
               waits=[("ve", vr)], inc="ve")
        se_sq = p.step("scalar",
                       lambda s, sl=sl: s.activation(out=sqs[:, :], in_=xtm[:, sl, :], func=AF.Square),
                       waits=[("se", se_tr)], inc="se")
        vr = p.step("vector",
                    lambda v: v.reduce_sum(stat[:, 1:2], sqs[:, :], axis=AX),
                    waits=[("se", se_sq)], inc="ve")
        def b(v):
            v.tensor_single_scalar(stat[:, 1:2], stat[:, 1:2], 1.0 / D, ALU.mult)
            return v.tensor_single_scalar(stat[:, 2:3], stat[:, 0:1], stat[:, 0:1], ALU.mult)
        p.step("vector", b, waits=[("ve", vr)], inc="ve")
        ve_var = p.step("vector",
                        lambda v: v.tensor_tensor(stat[:, 1:2], stat[:, 1:2], stat[:, 2:3],
                                                  ALU.subtract),
                        waits=(), inc="ve")
        se_std = p.step("scalar",
                        lambda s: s.activation(out=stat[:, 3:4], in_=stat[:, 1:2],
                                               func=AF.Sqrt, bias=cst[:, 0:1]),
                        waits=[("ve", ve_var)], inc="se")
        p.step("vector", lambda v: v.reciprocal(stat[:, 2:3], stat[:, 3:4]),
               waits=[("se", se_std)], inc="ve")
        ve_n = p.step("vector",
                      lambda v, sl=sl: v.tensor_scalar(sqs[:, :], xtm[:, sl, :], stat[:, 0:1],
                                                       stat[:, 2:3], ALU.subtract, ALU.mult),
                      waits=(), inc="ve")
        qw = [("ve", ve_n), ("pe", xq_trpe.get(tt - 2, 0))]
        ln_done[tt] = emit_quant(p, sqs[:, :], xq[:, sl, :], t["qscr"][:, 0:D], t["qscrP"][:, 0:D],
                                 t["qscrQ"][:, 0:D], t["qscrR"][:, 0:D], t["qm"], t["qr"], t["qs"],
                                 D, waits=qw)
        for half in range(2):
            w = [("ve", ln_done[tt]), ("se", p.val("se"))]
            def b(tn, sl=sl, half=half):
                last = None
                for j in range(4):
                    jj = half * 4 + j
                    last = tn.transpose(pb[j][:, 0:128],
                                        xq[:, sl, jj * 128:(jj + 1) * 128], identf_s[:, :])
                return last
            pe_tr = p.step("tensor", b, waits=w, inc="pe")
            xq_trpe[tt] = pe_tr
            def b(s, tt=tt, half=half):
                last = None
                for j in range(4):
                    last = s.copy(out=xqT[:, half * 4 + j, tt * 128:(tt + 1) * 128],
                                  in_=pb[j][:, 0:128])
                return last
            p.step("scalar", b, waits=[("pe", pe_tr)], inc="se")
    XQ_SE = p.val("se")
    if t.get("xqT_dbg") is not None:
        p.dmas("gpsimd", [
            lambda g: g.dma_start(out=t["xqT_dbg"].rearrange("(kt p) t -> p kt t", p=128),
                                  in_=xqT[:, :, :]),
        ], waits=[("se", XQ_SE)], sem="out")

    chunks = [(c * 512, 512) for c in range(5)] + [(2560, 256)]
    dn_free = {}
    for ci, (c0, cw) in enumerate(chunks):
        its = cw // 128
        dc_w = p.dmas("gpsimd", [
            lambda g, c0=c0, cw=cw: g.dma_start(
                out=gch[:, :, 0:cw],
                in_=gq_d.rearrange("(kt p) i -> p kt i", p=128)[:, :, c0:c0 + cw]),
            lambda g, c0=c0, cw=cw: g.dma_start(
                out=uch[:, :, 0:cw],
                in_=uq_d.rearrange("(kt p) i -> p kt i", p=128)[:, :, c0:c0 + cw]),
            lambda g, c0=c0, its=its: g.dma_start(
                out=dch[:, 0:its, :],
                in_=dq_d.rearrange("(a p) r -> p a r", p=128)[:, c0 // 128:c0 // 128 + its, :]),
        ], waits=[(t.get("ready_sem", "dx"), ready_dx),
                  ("pe", p.val("pe"))], sem="dc")

        hq_done = {}
        ve_h = 0
        se_si = 0
        for tt in range(ntiles):
            w = [("se", XQ_SE), ("dc", dc_w), ("ve", ve_h), ("se", se_si),
                 ("ve", max(dn_free.get(0, 0), dn_free.get(1, 0)))]
            def b(tn, tt=tt, cw=cw):
                last = None
                for kt in range(KT):
                    last = tn.matmul(pb[0][:, 0:cw], xqT[:, kt, tt * 128:(tt + 1) * 128],
                                     gch[:, kt, 0:cw], start=(kt == 0), stop=(kt == KT - 1))
                for kt in range(KT):
                    last = tn.matmul(pb[1][:, 0:cw], xqT[:, kt, tt * 128:(tt + 1) * 128],
                                     uch[:, kt, 0:cw], start=(kt == 0), stop=(kt == KT - 1))
                return last
            pe_gu = p.step("tensor", b, waits=w, inc="pe")
            se_si = p.step("scalar",
                           lambda s, cw=cw: s.activation(out=gsc[:, 0:cw], in_=pb[0][:, 0:cw],
                                                         func=AF.Silu),
                           waits=[("pe", pe_gu)], inc="se")
            ve_h = p.step("vector",
                          lambda v, cw=cw: v.tensor_tensor(hsc[:, 0:cw], pb[1][:, 0:cw],
                                                           gsc[:, 0:cw], ALU.mult),
                          waits=[("se", se_si)], inc="ve")
            hq_done[tt] = emit_quant(p, hsc[:, 0:cw], hq[:, 0:cw], t["qscr"][:, 0:cw],
                                     t["qscrP"][:, 0:cw], t["qscrQ"][:, 0:cw], t["qscrR"][:, 0:cw],
                                     t["qm"], t["qr"], t["qs"], cw, waits=[("ve", ve_h)])
            for j in range(its):
                w = [("ve", hq_done[tt]), ("se", p.val("se"))]
                pe_tr = p.step("tensor",
                               lambda tn, j=j: tn.transpose(pb[2 + (j % 2)][:, 0:128],
                                                            hq[:, j * 128:(j + 1) * 128],
                                                            identf_s[:, :]),
                               waits=w, inc="pe")
                p.step("scalar",
                       lambda s, tt=tt, j=j: s.copy(out=hqT[:, j, tt * 128:(tt + 1) * 128],
                                                    in_=pb[2 + (j % 2)][:, 0:128]),
                       waits=[("pe", pe_tr)], inc="se")
        HQ_SE = p.val("se")

        for dt in range(KT):
            for tci, (tc0, tcw) in enumerate(tcs):
                pbi = (dt * len(tcs) + tci) % 2
                w = [("se", HQ_SE), ("ve", dn_free.get(pbi, 0))]
                def b(tn, dt=dt, tc0=tc0, tcw=tcw, its=its, pbi=pbi):
                    last = None
                    for j in range(its):
                        last = tn.matmul(pb[pbi][:, 0:tcw], dch[:, j, dt * 128:(dt + 1) * 128],
                                         hqT[:, j, tc0:tc0 + tcw],
                                         start=(j == 0), stop=(j == its - 1))
                    return last
                pe_dn = p.step("tensor", b, waits=w, inc="pe")
                dn_free[pbi] = p.step(
                    "vector",
                    lambda v, dt=dt, tc0=tc0, tcw=tcw, pbi=pbi: v.tensor_tensor(
                        xT[:, dt, tc0:tc0 + tcw],
                        xT[:, dt, tc0:tc0 + tcw], pb[pbi][:, 0:tcw], ALU.add),
                    waits=[("pe", pe_dn)], inc="ve")


_L1 = None


def _get_l1():
    global _L1
    if _L1 is None:
        _L1 = build_L1().emit()
    return _L1


def _host_prep_l1(inputs):
    x0 = np.asarray(inputs["L0_attn_a_x"])
    wq_t = np.ascontiguousarray(np.asarray(inputs["L0_attn_a_Wq"]).T)
    wk_t = np.ascontiguousarray(np.asarray(inputs["L0_attn_a_Wk"]).T)
    wv_t = np.ascontiguousarray(np.asarray(inputs["L0_attn_a_Wv"]).T)
    wu_t = np.ascontiguousarray(np.asarray(inputs["L0_attn_a_Wu"]).T)
    bu = np.ascontiguousarray(np.asarray(inputs["L0_attn_a_bu"]).reshape(KT, 128).T)
    w_gate = np.ascontiguousarray(np.asarray(inputs["L0_ffn_w_gate"]))
    w_up = np.ascontiguousarray(np.asarray(inputs["L0_ffn_w_up"]))
    w_down = np.ascontiguousarray(np.asarray(inputs["L0_ffn_w_down"]))
    ident = np.eye(128, dtype=NPBF16)

    in_maps = []
    for c in range(8):
        b, q = c // 4, c % 4
        xb = x0[b]
        rot = np.concatenate([xb[q * TOK:(q + 1) * TOK], xb[:q * TOK], xb[(q + 1) * TOK:]], axis=0)
        in_maps.append(dict(
            x0_rot=np.ascontiguousarray(rot),
            x0T_own=np.ascontiguousarray(xb[q * TOK:(q + 1) * TOK].T),
            wq_t=wq_t, wk_t=wk_t, wv_t=wv_t, wu_t=wu_t, bu_in=bu,
            w_gate=w_gate, w_up=w_up, w_down=w_down, identb=ident,
        ))
    return in_maps


def _run_l1(inputs):
    nc = _get_l1()
    in_maps = _host_prep_l1(inputs)
    res = run_bass_kernel_spmd(nc, in_maps, core_ids=list(range(8)))
    return res


def kernel_3launch(**inputs):
    r1 = _run_l1(inputs)
    x2_full = np.empty((2, 8192, D), np.float32)
    for c in range(8):
        b, q = c // 4, c % 4
        x2_full[b, q * TOK:(q + 1) * TOK] = np.asarray(r1.results[c]["x2T_out"]).T
    ffnw = (np.asarray(r1.results[0]["gate_qt_out"]),
            np.asarray(r1.results[0]["up_qt_out"]),
            np.asarray(r1.results[0]["down_qt_out"]))
    r2 = _run_l2(inputs, x2_full)
    l2_results = [r2.results[c] for c in range(8)]
    r3 = _run_l3(inputs, l2_results, x2_full, ffnw)
    out = np.empty((2, 8192, D), np.float32)
    for c in range(8):
        b, q = c // 4, c % 4
        out[b, q * TOK:(q + 1) * TOK] = np.asarray(r3.results[c]["x4T_out"]).T
    return out


THAL = 3
TPAD = 2176
NTP = TPAD // 128


def build_L2():
    p = Prog()
    nc = p.nc

    x2_pad = nc.dram_tensor("x2_pad", [TPAD, D], F32, kind="ExternalInput").ap()
    w_in_t = nc.dram_tensor("w_in_t", [D, 2 * DI], F32, kind="ExternalInput").ap()
    wxp_t = nc.dram_tensor("wxp_t", [DI, 96], F32, kind="ExternalInput").ap()
    w_dt_t = nc.dram_tensor("w_dt_t", [DTR, DI], F32, kind="ExternalInput").ap()
    b_dt_in = nc.dram_tensor("b_dt_in", [128, DIT], F32, kind="ExternalInput").ap()
    a_log_in = nc.dram_tensor("a_log_in", [128, DIT * NST], F32, kind="ExternalInput").ap()
    conv_w_in = nc.dram_tensor("conv_w_in", [128, DIT * DCONV], F32, kind="ExternalInput").ap()
    conv_b_in = nc.dram_tensor("conv_b_in", [128, DIT], F32, kind="ExternalInput").ap()
    nw_rep_in = nc.dram_tensor("nw_rep_in", [128, D], F32, kind="ExternalInput").ap()
    identf_in = nc.dram_tensor("identf_in", [128, 128], F32, kind="ExternalInput").ap()

    delta_o = nc.dram_tensor("delta_o", [DI, TOK], F32, kind="ExternalOutput").ap()
    xs_o = nc.dram_tensor("xs_o", [DI, TOK], F32, kind="ExternalOutput").ap()
    sres_o = nc.dram_tensor("sres_o", [DI, TOK], F32, kind="ExternalOutput").ap()
    bc_o = nc.dram_tensor("bc_o", [32, TOK], F32, kind="ExternalOutput").ap()
    psum_o = nc.dram_tensor("psum_o", [DI, NST], F32, kind="ExternalOutput").ap()
    ssum_o = nc.dram_tensor("ssum_o", [DI, NST], F32, kind="ExternalOutput").ap()

    identf_s = p.sbuf("identf_s", [128, 128], F32)
    cst = p.sbuf("cst", [128, 2], F32)
    stat = p.sbuf("stat", [128, 8], F32)
    sqs = p.sbuf("sqs", [128, D], F32)
    nw_rep = p.sbuf("nw_rep", [128, D], F32)
    b_dt_s = p.sbuf("b_dt_s", [128, DIT], F32)
    a_s = p.sbuf("a_s", [128, DIT * NST], F32)
    cw_s = p.sbuf("cw_s", [128, DIT * DCONV], F32)
    cb_s = p.sbuf("cb_s", [128, DIT], F32)
    xn2T = p.sbuf("xn2T", [128, KT, TPAD], BF16)
    dbc = p.sbuf("dbc", [96, TOK], F32)
    dbcb = p.sbuf("dbcb", [64, TOK], BF16)
    dbcb2 = p.sbuf("dbcb2", [32, TOK], BF16)

    pb = [p.psum(f"pb{i}", [128, 512]) for i in range(2)]
    ps_x = [p.psum(f"ps_x{i}", [96, 512]) for i in range(4)]
    ps_d = [p.psum(f"ps_d{i}", [128, 512]) for i in range(2)]

    DXW = p.dmas("gpsimd", [
        lambda g: g.dma_start(out=nw_rep[:, :], in_=nw_rep_in),
        lambda g: g.dma_start(out=b_dt_s[:, :], in_=b_dt_in),
        lambda g: g.dma_start(out=a_s[:, :], in_=a_log_in),
        lambda g: g.dma_start(out=cw_s[:, :], in_=conv_w_in),
        lambda g: g.dma_start(out=cb_s[:, :], in_=conv_b_in),
    ])
    DWI = p.dmas("sync", [
        lambda sy: sy.dma_start(out=identf_s[:, :], in_=identf_in),
    ], sem="dw")
    p.step("vector", lambda v: v.memset(cst[:, 0:1], EPS), waits=[], inc="ve")
    se_a = p.step("scalar",
                  lambda s: s.activation(out=a_s[:, :], in_=a_s[:, :], func=AF.Exp),
                  waits=[("dx", DXW)], inc="se")
    VE_A = p.step("vector",
                  lambda v: v.tensor_single_scalar(a_s[:, :], a_s[:, :], -1.0, ALU.mult),
                  waits=[("se", se_a)], inc="ve")

    x2t = p.sbuf("x2t", [128, 2, D], F32)
    xnf2 = p.sbuf("xnf2", [128, 2, D], F32)
    hstg = p.sbuf("hstg", [4, D], F32)
    ln_done = {}
    for tt in range(NTP):
        sl = tt % 2
        dx_in = p.dmas("gpsimd", [
            lambda g, tt=tt, sl=sl: g.dma_start(out=x2t[:, sl, :],
                                                in_=x2_pad[tt * 128:(tt + 1) * 128, :]),
        ], waits=[("ve", ln_done.get(tt - 2, 0))])
        vr = p.step("vector",
                    lambda v, sl=sl: v.reduce_sum(stat[:, 0:1], x2t[:, sl, :], axis=AX),
                    waits=[("dx", dx_in)], inc="ve")
        p.step("vector",
               lambda v: v.tensor_single_scalar(stat[:, 0:1], stat[:, 0:1], 1.0 / D, ALU.mult),
               waits=[("ve", vr)], inc="ve")
        se_sq = p.step("scalar",
                       lambda s, sl=sl: s.activation(out=sqs[:, :], in_=x2t[:, sl, :], func=AF.Square),
                       waits=[("dx", dx_in)], inc="se")
        vr = p.step("vector", lambda v: v.reduce_sum(stat[:, 1:2], sqs[:, :], axis=AX),
                    waits=[("se", se_sq)], inc="ve")
        def b(v):
            v.tensor_single_scalar(stat[:, 1:2], stat[:, 1:2], 1.0 / D, ALU.mult)
            return v.tensor_single_scalar(stat[:, 2:3], stat[:, 0:1], stat[:, 0:1], ALU.mult)
        p.step("vector", b, waits=[("ve", vr)], inc="ve")
        ve_var = p.step("vector",
                        lambda v: v.tensor_tensor(stat[:, 1:2], stat[:, 1:2], stat[:, 2:3],
                                                  ALU.subtract),
                        waits=(), inc="ve")
        se_std = p.step("scalar",
                        lambda s: s.activation(out=stat[:, 3:4], in_=stat[:, 1:2], func=AF.Sqrt,
                                               bias=cst[:, 0:1]),
                        waits=[("ve", ve_var)], inc="se")
        p.step("vector", lambda v: v.reciprocal(stat[:, 2:3], stat[:, 3:4]),
               waits=[("se", se_std)], inc="ve")
        ve_ln = p.step("vector",
                       lambda v, sl=sl: v.tensor_scalar(xnf2[:, sl, :], x2t[:, sl, :],
                                                        stat[:, 0:1], stat[:, 2:3],
                                                        ALU.subtract, ALU.mult),
                       waits=(), inc="ve")
        se_sq2 = p.step("scalar",
                        lambda s, sl=sl: s.activation(out=sqs[:, :], in_=xnf2[:, sl, :],
                                                      func=AF.Square),
                        waits=[("ve", ve_ln)], inc="se")
        vr = p.step("vector", lambda v: v.reduce_sum(stat[:, 4:5], sqs[:, :], axis=AX),
                    waits=[("se", se_sq2)], inc="ve")
        p.step("vector",
               lambda v: v.tensor_single_scalar(stat[:, 4:5], stat[:, 4:5], 1.0 / D, ALU.mult),
               waits=[("ve", vr)], inc="ve")
        se_std2 = p.step("scalar",
                         lambda s: s.activation(out=stat[:, 5:6], in_=stat[:, 4:5], func=AF.Sqrt,
                                                bias=cst[:, 0:1]),
                         waits=[("ve", p.val("ve"))], inc="se")
        p.step("vector", lambda v: v.reciprocal(stat[:, 6:7], stat[:, 5:6]),
               waits=[("se", se_std2)], inc="ve")
        p.step("vector",
               lambda v, sl=sl: v.tensor_single_scalar(xnf2[:, sl, :], xnf2[:, sl, :],
                                                       stat[:, 6:7], ALU.mult),
               waits=(), inc="ve")
        ln_done[tt] = p.step("vector",
                             lambda v, sl=sl: v.tensor_tensor(xnf2[:, sl, :], xnf2[:, sl, :],
                                                              nw_rep[:, :], ALU.mult),
                             waits=(), inc="ve")
        for half in range(2):
            w = [("ve", ln_done[tt]), ("dw", DWI), ("se", p.val("se"))]
            def b(tn, sl=sl, half=half):
                last = None
                for j in range(4):
                    jj = half * 4 + j
                    last = tn.transpose(pb[jj % 2][:, 0:128] if False else
                                        (pb[0] if j % 2 == 0 else pb[1])[:, 0:128],
                                        xnf2[:, sl, jj * 128:(jj + 1) * 128], identf_s[:, :])
                return last
            for j in range(4):
                jj = half * 4 + j
                pe_tr = p.step("tensor",
                               lambda tn, sl=sl, jj=jj, j=j: tn.transpose(
                                   pb[j % 2][:, 0:128], xnf2[:, sl, jj * 128:(jj + 1) * 128],
                                   identf_s[:, :]),
                               waits=w + [("se", p.val("se"))], inc="pe")
                p.step("scalar",
                       lambda s, tt=tt, jj=jj, j=j: s.copy(out=xn2T[:, jj, tt * 128:(tt + 1) * 128],
                                                           in_=pb[j % 2][:, 0:128]),
                       waits=[("pe", pe_tr)], inc="se")
    LN_SE = p.val("se")

    w_in_s = p.sbuf("w_in_s", [128, KT, 2 * DI], BF16)
    wxp_s = p.sbuf("wxp_s", [128, DIT, 96], BF16)
    wdt_s = p.sbuf("wdt_s", [64, DI], BF16)
    xcf = p.sbuf("xcf", [128, TPAD], F32)
    cacc = p.sbuf("cacc", [128, TOK], F32)
    xst = p.sbuf("xst", [128, TOK], F32)
    xsb = p.sbuf("xsb", [128, TOK], BF16)
    srt = p.sbuf("srt", [128, TOK], F32)

    DXW2 = p.dmas("gpsimd", [
        lambda g: g.dma_start(out=w_in_s[:, :, :], in_=w_in_t.rearrange("(kt p) r -> p kt r", p=128)),
        lambda g: g.dma_start(out=wxp_s[:, :, :], in_=wxp_t.rearrange("(dt p) c -> p dt c", p=128)),
        lambda g: g.dma_start(out=wdt_s[:, :], in_=w_dt_t),
    ])

    CH = [(c * 512, 512) for c in range(4)] + [(2048, 128)]
    xs_dma = {}
    for dt in range(DIT):
        for ci, (c0, cw) in enumerate(CH):
            w = [("se", LN_SE), ("dx", DXW2), ("se", p.val("se")), ("ve", p.val("ve"))]
            def b(tn, dt=dt, c0=c0, cw=cw, ci=ci):
                last = None
                for kt in range(KT):
                    last = tn.matmul(pb[ci % 2][:, 0:cw], w_in_s[:, kt, dt * 128:(dt + 1) * 128],
                                     xn2T[:, kt, c0:c0 + cw], start=(kt == 0), stop=(kt == KT - 1))
                return last
            pe_m = p.step("tensor", b, waits=w, inc="pe")
            p.step("vector",
                   lambda v, c0=c0, cw=cw, ci=ci: v.tensor_copy(xcf[:, c0:c0 + cw],
                                                                pb[ci % 2][:, 0:cw]),
                   waits=[("pe", pe_m)], inc="ve")
        def b(v, dt=dt):
            last = v.tensor_single_scalar(cacc[:, :], xcf[:, 125:125 + TOK],
                                          cw_s[:, dt * 4:dt * 4 + 1], ALU.mult)
            for k in range(1, DCONV):
                last = v.scalar_tensor_tensor(cacc[:, :], xcf[:, 125 + k:125 + k + TOK],
                                              cw_s[:, dt * 4 + k:dt * 4 + k + 1], cacc[:, :],
                                              ALU.mult, ALU.add)
            return last
        ve_cv = p.step("vector", b, waits=(), inc="ve")
        se_xs = p.step("scalar",
                       lambda s, dt=dt: s.activation(out=xst[:, :], in_=cacc[:, :], func=AF.Silu,
                                                     bias=cb_s[:, dt:dt + 1]),
                       waits=[("ve", ve_cv), ("dx", xs_dma.get(dt - 1, 0))], inc="se")
        ve_xsb = p.step("vector", lambda v: v.tensor_copy(xsb[:, :], xst[:, :]),
                        waits=[("se", se_xs)], inc="ve")
        xs_dma[dt] = p.dmas("gpsimd", [
            lambda g, dt=dt: g.dma_start(out=xs_o[dt * 128:(dt + 1) * 128, :], in_=xst[:, :]),
        ], waits=[("se", se_xs), ("ve", ve_xsb)])
        def b(tn, dt=dt):
            last = None
            for tc in range(4):
                last = tn.matmul(ps_x[tc][0:96, :], wxp_s[:, dt, :],
                                 xsb[:, tc * 512:(tc + 1) * 512],
                                 start=(dt == 0), stop=(dt == DIT - 1))
            return last
        p.step("tensor", b, waits=[("ve", ve_xsb)], inc="pe")
        for tc in range(4):
            w = [("se", p.val("se"), ), ("ve", p.val("ve"))]
            def b(tn, dt=dt, tc=tc):
                last = None
                for kt in range(KT):
                    last = tn.matmul(ps_d[tc % 2][:, :], w_in_s[:, kt, DI + dt * 128:DI + (dt + 1) * 128],
                                     xn2T[:, kt, 128 + tc * 512:128 + (tc + 1) * 512],
                                     start=(kt == 0), stop=(kt == KT - 1))
                return last
            pe_r = p.step("tensor", b, waits=w, inc="pe")
            p.step("scalar",
                   lambda s, tc=tc: s.activation(out=srt[:, tc * 512:(tc + 1) * 512],
                                                 in_=ps_d[tc % 2][:, :], func=AF.Silu),
                   waits=[("pe", pe_r), ("dx", p.mem.get("_sres_dma", 0))], inc="se")
        p.mem["_sres_dma"] = p.dmas("gpsimd", [
            lambda g, dt=dt: g.dma_start(out=sres_o[dt * 128:(dt + 1) * 128, :], in_=srt[:, :]),
        ], waits=[("se", p.val("se"))])
    XP_PE = p.val("pe")

    def b(v):
        last = None
        for tc in range(4):
            last = v.tensor_copy(dbc[0:96, tc * 512:(tc + 1) * 512], ps_x[tc][0:96, :])
        return last
    ve_dbc = p.step("vector", b, waits=[("pe", XP_PE)], inc="ve")
    ve_dbcb = p.step("vector", lambda v: v.tensor_copy(dbcb[0:64, :], dbc[0:64, :]),
                     waits=[("ve", ve_dbc)], inc="ve")
    p.dmas("gpsimd", [
        lambda g: g.dma_start(out=bc_o, in_=dbc[64:96, :]),
    ], waits=[("ve", ve_dbc)])

    dl1 = p.sbuf("dl1", [128, TOK], F32)
    dl2 = p.sbuf("dl2", [128, 512], F32)
    td = p.sbuf("td", [128, 1], F32)
    pp = p.sbuf("pp", [128, NST], F32)
    for dt in range(DIT):
        for tc in range(4):
            w = [("ve", ve_dbcb), ("se", p.val("se")), ("ve", p.val("ve"))]
            def b(tn, dt=dt, tc=tc):
                return tn.matmul(ps_d[tc % 2][:, :], wdt_s[0:64, dt * 128:(dt + 1) * 128],
                                 dbcb[0:64, tc * 512:(tc + 1) * 512], start=True, stop=True)
            pe_d = p.step("tensor", b, waits=w, inc="pe")
            def b(s, dt=dt, tc=tc):
                return s.activation(out=dl2[:, :], in_=ps_d[tc % 2][:, :], func=AF.Abs,
                                    bias=b_dt_s[:, dt:dt + 1])
            se1 = p.step("scalar", b, waits=[("pe", pe_d), ("dx", p.mem.get("_dl_dma", 0))], inc="se")
            se2 = p.step("scalar",
                         lambda s: s.activation(out=dl2[:, :], in_=dl2[:, :], func=AF.Exp,
                                                scale=-1.0),
                         waits=(), inc="se")
            se3 = p.step("scalar",
                         lambda s: s.activation(out=dl2[:, :], in_=dl2[:, :], func=AF.Ln,
                                                bias=1.0),
                         waits=(), inc="se")
            se4 = p.step("scalar",
                         lambda s, dt=dt, tc=tc: s.activation(out=dl1[:, tc * 512:(tc + 1) * 512],
                                                              in_=ps_d[tc % 2][:, :], func=AF.Relu,
                                                              bias=b_dt_s[:, dt:dt + 1]),
                         waits=(), inc="se")
            p.step("vector",
                   lambda v, tc=tc: v.tensor_tensor(dl1[:, tc * 512:(tc + 1) * 512],
                                                    dl1[:, tc * 512:(tc + 1) * 512],
                                                    dl2[:, :], ALU.add),
                   waits=[("se", se4), ("dx", p.mem.get("_dl_dma", 0))], inc="ve")
        vr = p.step("vector", lambda v: v.reduce_sum(td[:, :], dl1[:, :], axis=AX),
                    waits=[("dx", p.mem.get("_dl_dma", 0))], inc="ve")
        p.step("vector",
               lambda v, dt=dt: v.tensor_single_scalar(pp[:, :], a_s[:, dt * NST:(dt + 1) * NST],
                                                       td[:, 0:1], ALU.mult),
               waits=[("ve", vr), ("ve", VE_A)], inc="ve")
        se_p = p.step("scalar", lambda s: s.activation(out=pp[:, :], in_=pp[:, :], func=AF.Exp),
                      waits=[("ve", p.val("ve"))], inc="se")
        p.mem["_dl_dma"] = p.dmas("gpsimd", [
            lambda g, dt=dt: g.dma_start(out=delta_o[dt * 128:(dt + 1) * 128, :], in_=dl1[:, :]),
            lambda g, dt=dt: g.dma_start(out=psum_o[dt * 128:(dt + 1) * 128, :], in_=pp[:, :]),
        ], waits=[("se", se_p), ("ve", p.val("ve"))])
    PRE_DX = p.val("dx")

    p.barrier()
    p.release("pp", "td", "dl2", "dl1", "srt", "xsb", "xst", "cacc", "xcf",
              "wdt_s", "wxp_s", "w_in_s")
    p.psum_release("pb0", "pb1", "ps_x0", "ps_x1", "ps_x2", "ps_x3", "ps_d0", "ps_d1")
    emit_scan(p, dict(identf=identf_s, a_s=a_s, dbc=None,
                      delta_d=delta_o, xs_d=xs_o, bc_d=bc_o,
                      hinit=None, ssum_o=ssum_o, post=None))

    p.step("gpsimd", lambda g: None, waits=[("out", p.val("out")), ("dx", p.val("dx"))])
    return p


def emit_scan(p, t):
    a_s = t["a_s"]
    delta_d, xs_d, bc_d = t["delta_d"], t["xs_d"], t["bc_d"]
    dlt = p.sbuf("sc_dl", [128, 2, TOK], F32)
    xs2 = p.sbuf("sc_xs", [128, 2, TOK], F32)
    ut = p.sbuf("sc_u", [128, TOK], F32)
    rowb = p.sbuf("sc_row", [1, 2, 2, TOK], BF16)
    dat = p.sbuf("sc_da", [128, 2, TOK], F32)
    dbu = p.sbuf("sc_dbu", [128, TOK], F32)
    ht = p.sbuf("sc_h", [128, TOK], F32)
    yac = p.sbuf("sc_y", [128, TOK], F32)
    ssb = p.sbuf("sc_s", [128, NST], F32)
    ones1 = p.sbuf("sc_one", [1, 128], BF16)
    hin = p.sbuf("sc_hi", [128, NST], F32) if t.get("hinit") is not None else None
    pbB = p.psum("pbB", [128, 2048])
    pbC = p.psum("pbC", [128, 2048])
    t["_scan_names"] = [n for n in ["sc_hi" if hin is not None else None, "sc_one",
                                    "sc_s", "sc_y", "sc_h", "sc_dbu", "sc_da",
                                    "sc_row", "sc_u", "sc_xs", "sc_dl"] if n]

    ve_one = p.step("vector", lambda v: v.memset(ones1[:, :], 1.0), waits=(), inc="ve")

    bc_bf = t.get("bc_bf", bc_d)
    post = t.get("post")
    dma_done = {}
    pe_B = pe_C = 0
    ve_dbu = ve_cm = ve_one
    row_used = {}
    for dt in range(DIT):
        sl = dt % 2
        w = [("ve", p.val("ve"))] if dt >= 2 else []
        dx_in = p.dmas("gpsimd", [
            lambda g, dt=dt, sl=sl: g.dma_start(out=dlt[:, sl, :],
                                                in_=delta_d[dt * 128:(dt + 1) * 128, :]),
            lambda g, dt=dt, sl=sl: g.dma_start(out=xs2[:, sl, :],
                                                in_=xs_d[dt * 128:(dt + 1) * 128, :]),
        ], waits=w)
        if hin is not None:
            dxh = t["hinit"](p, dt, hin)
        ve_u = p.step("vector",
                      lambda v, sl=sl: v.tensor_tensor(ut[:, :], dlt[:, sl, :], xs2[:, sl, :],
                                                       ALU.mult),
                      waits=[("dx", dx_in)], inc="ve")
        for n in range(NST):
            bsl = n % 2
            dx_b = p.dmas("gpsimd", [
                lambda g, n=n, bsl=bsl: g.dma_start(out=rowb[0:1, 0, bsl, :],
                                                    in_=bc_bf[n:n + 1, :]),
                lambda g, n=n, bsl=bsl: g.dma_start(out=rowb[0:1, 1, bsl, :],
                                                    in_=bc_bf[16 + n:17 + n, :]),
            ], waits=[("pe", row_used.get(bsl, 0))], sem="dr")
            def bB(tn, bsl=bsl):
                last = None
                for c in range(4):
                    last = tn.matmul(pbB[:, c * 512:(c + 1) * 512], ones1[0:1, :],
                                     rowb[0:1, 0, bsl, c * 512:(c + 1) * 512],
                                     start=True, stop=True)
                return last
            pe_B = p.step("tensor", bB, waits=[("dr", dx_b), ("ve", ve_dbu)], inc="pe")
            def bC(tn, bsl=bsl):
                last = None
                for c in range(4):
                    last = tn.matmul(pbC[:, c * 512:(c + 1) * 512], ones1[0:1, :],
                                     rowb[0:1, 1, bsl, c * 512:(c + 1) * 512],
                                     start=True, stop=True)
                return last
            pe_C = p.step("tensor", bC, waits=[("ve", ve_cm)], inc="pe")
            row_used[bsl] = pe_C
            dsl = n % 2
            se_da = p.step("scalar",
                           lambda s, dt=dt, n=n, sl=sl, dsl=dsl: s.activation(
                               out=dat[:, dsl, :], in_=dlt[:, sl, :], func=AF.Exp,
                               scale=a_s[:, dt * NST + n:dt * NST + n + 1]),
                           waits=[("dx", dx_in), ("se", p.val("se")),
                                  ("ve", p.mem.get(("_scan_ve_h", dsl), 0))], inc="se")
            ve_dbu = p.step("vector",
                            lambda v: v.tensor_tensor(dbu[:, :], ut[:, :],
                                                      pbB[:, :], ALU.mult),
                            waits=[("ve", ve_u), ("pe", pe_B)], inc="ve")
            init = 0.0 if hin is None else hin[:, n:n + 1]
            ve_h = p.step("vector",
                          lambda v, init=init, dsl=dsl: v.tensor_tensor_scan(
                              ht[:, :], dat[:, dsl, :], dbu[:, :],
                              init, ALU.mult, ALU.add),
                          waits=[("se", se_da), ("ve", ve_dbu)], inc="ve")
            p.mem[("_scan_ve_h", dsl)] = ve_h
            wS = [("ve", ve_h)]
            if n == 0 and p.mem.get("_ssb_war"):
                wS.append(p.mem["_ssb_war"])
            p.step("vector",
                   lambda v, n=n: v.tensor_copy(ssb[:, n:n + 1], ht[:, TOK - 1:TOK]),
                   waits=wS, inc="ve")
            ve_cm = p.step("vector",
                           lambda v: v.tensor_tensor(ht[:, :], ht[:, :], pbC[:, :],
                                                     ALU.mult),
                           waits=[("pe", pe_C)], inc="ve")
            if n == 0:
                ve_y = p.step("vector", lambda v: v.tensor_copy(yac[:, :], ht[:, :]),
                              waits=[("dx", p.mem.get("_yac_war", 0))], inc="ve")
            else:
                ve_y = p.step("vector",
                              lambda v: v.tensor_tensor(yac[:, :], yac[:, :], ht[:, :], ALU.add),
                              waits=(), inc="ve")
        ss_slice = t.get("ssum_slice", lambda d: t["ssum_o"][d * 128:(d + 1) * 128, :])
        dma_done[dt] = p.dmas("gpsimd", [
            lambda g, dt=dt: g.dma_start(out=ss_slice(dt), in_=ssb[:, :]),
        ], waits=[("ve", p.val("ve"))], sem=t.get("ssum_sem", "dx"))
        p.mem["_ssb_war"] = (t.get("ssum_sem", "dx"), dma_done[dt])
        if post is not None:
            post(p, dt, dict(yac=yac, xs2=xs2, sl=sl))
    p.barrier()
    p.psum_release("pbC", "pbB")


_L2 = None


def _get_l2():
    global _L2
    if _L2 is None:
        _L2 = build_L2().emit()
    return _L2


def _host_prep_l2(inputs, x2_full):
    W_in = np.asarray(inputs["L0_attn_b_W_in"])
    w_in_t = np.ascontiguousarray(W_in.T)
    W_xp = np.asarray(inputs["L0_attn_b_W_xproj"])
    wxp_t = np.ascontiguousarray(W_xp.T)
    w_dt_t = np.ascontiguousarray(np.asarray(inputs["L0_attn_b_W_dt"]).T)
    b_dt = np.ascontiguousarray(np.asarray(inputs["L0_attn_b_b_dt"]).reshape(DIT, 128).T)
    A_log = np.asarray(inputs["L0_attn_b_A_log"])
    a_in = np.ascontiguousarray(A_log.reshape(DIT, 128, NST).transpose(1, 0, 2).reshape(128, DIT * NST))
    conv_w = np.ascontiguousarray(np.asarray(inputs["L0_attn_b_conv_w"])
                                  .reshape(DIT, 128, DCONV).transpose(1, 0, 2).reshape(128, DIT * DCONV))
    conv_b = np.ascontiguousarray(np.asarray(inputs["L0_attn_b_conv_b"]).reshape(DIT, 128).T)
    nw_rep = np.ascontiguousarray(np.broadcast_to(np.asarray(inputs["L0_attn_b_norm_w"])[None, :], (128, D)))
    identf = np.eye(128, dtype=np.float32)

    in_maps = []
    for c in range(8):
        b, q = c // 4, c % 4
        xb = x2_full[b]
        pad = np.zeros((TPAD, D), np.float32)
        if q > 0:
            pad[125:128] = xb[q * TOK - 3:q * TOK]
        pad[128:128 + TOK] = xb[q * TOK:(q + 1) * TOK]
        in_maps.append(dict(
            x2_pad=pad, w_in_t=w_in_t, wxp_t=wxp_t, w_dt_t=w_dt_t, b_dt_in=b_dt,
            a_log_in=a_in, conv_w_in=conv_w, conv_b_in=conv_b, nw_rep_in=nw_rep,
            identf_in=identf,
        ))
    return in_maps


def _run_l2(inputs, x2_full):
    nc = _get_l2()
    return run_bass_kernel_spmd(nc, _host_prep_l2(inputs, x2_full), core_ids=list(range(8)))


def build_L3():
    p = Prog()
    nc = p.nc

    delta_in = nc.dram_tensor("delta_in", [DI, TOK], F32, kind="ExternalInput").ap()
    xs_in = nc.dram_tensor("xs_in", [DI, TOK], F32, kind="ExternalInput").ap()
    sres_in = nc.dram_tensor("sres_in", [DI, TOK], F32, kind="ExternalInput").ap()
    bc_in = nc.dram_tensor("bc_in", [32, TOK], F32, kind="ExternalInput").ap()
    x2T_in = nc.dram_tensor("x2T_in", [D, TOK], F32, kind="ExternalInput").ap()
    x2tm_in = nc.dram_tensor("x2tm_in", [TOK, D], F32, kind="ExternalInput").ap()
    s1_in = nc.dram_tensor("s1_in", [DI, NST], F32, kind="ExternalInput").ap()
    p1_in = nc.dram_tensor("p1_in", [DI, NST], F32, kind="ExternalInput").ap()
    s2_in = nc.dram_tensor("s2_in", [DI, NST], F32, kind="ExternalInput").ap()
    p2_in = nc.dram_tensor("p2_in", [DI, NST], F32, kind="ExternalInput").ap()
    s3_in = nc.dram_tensor("s3_in", [DI, NST], F32, kind="ExternalInput").ap()
    a_log_in = nc.dram_tensor("a_log_in", [128, DIT * NST], F32, kind="ExternalInput").ap()
    d_in = nc.dram_tensor("d_in", [128, DIT], F32, kind="ExternalInput").ap()
    w_out_t_in = nc.dram_tensor("w_out_t_in", [DI, D], F32, kind="ExternalInput").ap()
    gate_qt_in = nc.dram_tensor("gate_qt_in", [D, IF], BF16, kind="ExternalInput").ap()
    up_qt_in = nc.dram_tensor("up_qt_in", [D, IF], BF16, kind="ExternalInput").ap()
    down_qt_in = nc.dram_tensor("down_qt_in", [IF, D], BF16, kind="ExternalInput").ap()
    identf_in = nc.dram_tensor("identf_in", [128, 128], F32, kind="ExternalInput").ap()

    x4T_out = nc.dram_tensor("x4T_out", [D, TOK], F32, kind="ExternalOutput").ap()
    ssum_d = nc.dram_tensor("ssum_d", [DI, NST], F32).ap()
    yb_d = nc.dram_tensor("yb_d", [DI, TOK], BF16).ap()

    identf_s = p.sbuf("identf_s", [128, 128], F32)
    cst = p.sbuf("cst", [128, 2], F32)
    stat = p.sbuf("stat", [128, 8], F32)
    sqs = p.sbuf("sqs", [128, D], F32)
    a_s = p.sbuf("a_s", [128, DIT * NST], F32)
    d2 = p.sbuf("d2", [128, DIT], F32)
    x3T = p.sbuf("x3T", [128, KT, TOK], F32)

    DXW = p.dmas("gpsimd", [
        lambda g: g.dma_start(out=a_s[:, :], in_=a_log_in),
        lambda g: g.dma_start(out=d2[:, :], in_=d_in),
        lambda g: g.dma_start(out=x3T[:, :, :], in_=x2T_in.rearrange("(kt p) t -> p kt t", p=128)),
    ])
    DWI = p.dmas("sync", [
        lambda sy: sy.dma_start(out=identf_s[:, :], in_=identf_in),
    ], sem="dw")
    p.step("vector", lambda v: v.memset(cst[:, 0:1], EPS), waits=[], inc="ve")
    se_a = p.step("scalar", lambda s: s.activation(out=a_s[:, :], in_=a_s[:, :], func=AF.Exp),
                  waits=[("dx", DXW)], inc="se")
    VE_A = p.step("vector", lambda v: v.tensor_single_scalar(a_s[:, :], a_s[:, :], -1.0, ALU.mult),
                  waits=[("se", se_a)], inc="ve")

    sres_t = p.sbuf("sres_t", [128, 2, TOK], F32)
    ybt = p.sbuf("ybt", [128, TOK], BF16)
    hi_parts = p.sbuf("hi_parts", [128, 5, NST], F32)
    sres_dma = {}
    yb_dma = {}

    def hinit_hook(p, dt, hin):
        dxh = p.dmas("gpsimd", [
            lambda g, dt=dt: g.dma_start(out=hi_parts[:, 0, :], in_=s1_in[dt * 128:(dt + 1) * 128, :]),
            lambda g, dt=dt: g.dma_start(out=hi_parts[:, 1, :], in_=p1_in[dt * 128:(dt + 1) * 128, :]),
            lambda g, dt=dt: g.dma_start(out=hi_parts[:, 2, :], in_=s2_in[dt * 128:(dt + 1) * 128, :]),
            lambda g, dt=dt: g.dma_start(out=hi_parts[:, 3, :], in_=p2_in[dt * 128:(dt + 1) * 128, :]),
            lambda g, dt=dt: g.dma_start(out=hi_parts[:, 4, :], in_=s3_in[dt * 128:(dt + 1) * 128, :]),
        ], waits=[("ve", p.mem.get("_hi_used", 0))])
        def b(v):
            v.tensor_tensor(hin[:, :], hi_parts[:, 3, :], hi_parts[:, 4, :], ALU.mult)
            return v.tensor_tensor(hin[:, :], hin[:, :], hi_parts[:, 2, :], ALU.add)
        p.step("vector", b, waits=[("dx", dxh)], inc="ve")
        def b(v):
            v.tensor_tensor(hin[:, :], hin[:, :], hi_parts[:, 1, :], ALU.mult)
            return v.tensor_tensor(hin[:, :], hin[:, :], hi_parts[:, 0, :], ALU.add)
        p.mem["_hi_used"] = p.step("vector", b, waits=(), inc="ve")
        return dxh

    def post_hook(p, dt, env):
        yac, xs2, sl = env["yac"], env["xs2"], env["sl"]
        ssl = dt % 2
        sres_dma[dt] = p.dmas("gpsimd", [
            lambda g, dt=dt, ssl=ssl: g.dma_start(out=sres_t[:, ssl, :],
                                                  in_=sres_in[dt * 128:(dt + 1) * 128, :]),
        ], waits=[("ve", p.mem.get("_sres_used", 0))])
        p.step("vector",
               lambda v, dt=dt, sl=sl: v.scalar_tensor_tensor(yac[:, :], xs2[:, sl, :],
                                                              d2[:, dt:dt + 1], yac[:, :],
                                                              ALU.mult, ALU.add),
               waits=(), inc="ve")
        p.mem["_sres_used"] = p.step(
            "vector",
            lambda v, ssl=ssl: v.tensor_tensor(yac[:, :], yac[:, :], sres_t[:, ssl, :], ALU.mult),
            waits=[("dx", sres_dma[dt])], inc="ve")
        ve_yb = p.step("vector",
                       lambda v: v.tensor_copy(ybt[:, :], yac[:, :]),
                       waits=[("dx", yb_dma.get(dt - 1, 0))], inc="ve")
        yb_dma[dt] = p.dmas("gpsimd", [
            lambda g, dt=dt: g.dma_start(out=yb_d[dt * 128:(dt + 1) * 128, :], in_=ybt[:, :]),
        ], waits=[("ve", ve_yb)])

    scan_t = dict(identf=identf_s, a_s=a_s,
                  delta_d=delta_in, xs_d=xs_in, bc_d=bc_in,
                  hinit=hinit_hook, ssum_o=ssum_d, post=post_hook)
    emit_scan(p, scan_t)
    SCAN_DX = p.val("dx")
    pb = [p.psum(f"pb{i}", [128, 512]) for i in range(4)]

    p.barrier()
    p.release(*scan_t["_scan_names"])
    p.release("hi_parts", "ybt", "sres_t")
    w_out_s = p.sbuf("w_out_s", [128, DIT, D], BF16)
    ybtc = p.sbuf("ybtc", [128, DIT, 512], BF16)
    p.dmas("gpsimd", [
        lambda g: g.dma_start(out=w_out_s[:, :, :],
                              in_=w_out_t_in.rearrange("(dt p) o -> p dt o", p=128)),
    ])
    WO_DX0 = p.val("dx")
    dn_free = {}
    for tc in range(4):
        dx_y = p.dmas("gpsimd", [
            lambda g, tc=tc: g.dma_start(
                out=ybtc[:, :, :],
                in_=yb_d.rearrange("(dt p) t -> p dt t", p=128)[:, :, tc * 512:(tc + 1) * 512]),
        ], waits=[("pe", p.val("pe"))])
        for ot in range(KT):
            pbi = ot % 2
            w = [("dx", dx_y), ("dx", WO_DX0), ("ve", dn_free.get(pbi, 0))]
            def b(tn, tc=tc, ot=ot, pbi=pbi):
                last = None
                for dt in range(DIT):
                    last = tn.matmul(pb[pbi][:, :], w_out_s[:, dt, ot * 128:(ot + 1) * 128],
                                     ybtc[:, dt, :], start=(dt == 0), stop=(dt == DIT - 1))
                return last
            pe_w = p.step("tensor", b, waits=w, inc="pe")
            dn_free[pbi] = p.step(
                "vector",
                lambda v, tc=tc, ot=ot, pbi=pbi: v.tensor_tensor(
                    x3T[:, ot, tc * 512:(tc + 1) * 512], x3T[:, ot, tc * 512:(tc + 1) * 512],
                    pb[pbi][:, :], ALU.add),
                waits=[("pe", pe_w)], inc="ve")

    p.barrier()
    p.release("ybtc", "w_out_s")
    x2t = p.sbuf("x2t_l3", [128, 2, D], F32)
    lnf = p.sbuf("lnf_l3", [128, 2, D], F32)
    ln_done = {}
    for tt in range(NTO):
        sl = tt % 2
        dx_in = p.dmas("gpsimd", [
            lambda g, tt=tt, sl=sl: g.dma_start(out=x2t[:, sl, :],
                                                in_=x2tm_in[tt * 128:(tt + 1) * 128, :]),
        ], waits=[("ve", ln_done.get(tt - 2, 0))])
        vr = p.step("vector",
                    lambda v, sl=sl: v.reduce_sum(stat[:, 0:1], x2t[:, sl, :], axis=AX),
                    waits=[("dx", dx_in)], inc="ve")
        p.step("vector",
               lambda v: v.tensor_single_scalar(stat[:, 0:1], stat[:, 0:1], 1.0 / D, ALU.mult),
               waits=[("ve", vr)], inc="ve")
        se_sq = p.step("scalar",
                       lambda s, sl=sl: s.activation(out=sqs[:, :], in_=x2t[:, sl, :], func=AF.Square),
                       waits=[("dx", dx_in)], inc="se")
        vr = p.step("vector", lambda v: v.reduce_sum(stat[:, 1:2], sqs[:, :], axis=AX),
                    waits=[("se", se_sq)], inc="ve")
        def b(v):
            v.tensor_single_scalar(stat[:, 1:2], stat[:, 1:2], 1.0 / D, ALU.mult)
            return v.tensor_single_scalar(stat[:, 2:3], stat[:, 0:1], stat[:, 0:1], ALU.mult)
        p.step("vector", b, waits=[("ve", vr)], inc="ve")
        ve_var = p.step("vector",
                        lambda v: v.tensor_tensor(stat[:, 1:2], stat[:, 1:2], stat[:, 2:3],
                                                  ALU.subtract),
                        waits=(), inc="ve")
        se_std = p.step("scalar",
                        lambda s: s.activation(out=stat[:, 3:4], in_=stat[:, 1:2], func=AF.Sqrt,
                                               bias=cst[:, 0:1]),
                        waits=[("ve", ve_var)], inc="se")
        p.step("vector", lambda v: v.reciprocal(stat[:, 2:3], stat[:, 3:4]),
               waits=[("se", se_std)], inc="ve")
        ln_done[tt] = p.step("vector",
                             lambda v, sl=sl: v.tensor_scalar(lnf[:, sl, :], x2t[:, sl, :],
                                                              stat[:, 0:1], stat[:, 2:3],
                                                              ALU.subtract, ALU.mult),
                             waits=(), inc="ve")
        for j in range(KT):
            pe_tr = p.step("tensor",
                           lambda tn, sl=sl, j=j: tn.transpose(pb[2 + j % 2][:, 0:128],
                                                               lnf[:, sl, j * 128:(j + 1) * 128],
                                                               identf_s[:, :]),
                           waits=[("ve", ln_done[tt]), ("dw", DWI),
                                  ("ve", p.val("ve")), ("se", p.val("se"))], inc="pe")
            p.step("vector",
                   lambda v, tt=tt, j=j: v.tensor_tensor(x3T[:, j, tt * 128:(tt + 1) * 128],
                                                         x3T[:, j, tt * 128:(tt + 1) * 128],
                                                         pb[2 + j % 2][:, 0:128], ALU.add),
                   waits=[("pe", pe_tr)], inc="ve")
    X3_VE = p.val("ve")

    p.barrier()
    p.release("lnf_l3", "x2t_l3")
    qscr = p.sbuf("qscrF", [128, D], F32)
    qscrP = p.sbuf("qscrPF", [128, D], F32)
    qscrQ = p.sbuf("qscrQF", [128, D], F32)
    qscrR = p.sbuf("qscrRF", [128, D], F32)
    qm = p.sbuf("qmF", [128, D // QBLK], F32)
    qr = p.sbuf("qrF", [128, D // QBLK], F32)
    qs = p.sbuf("qsF", [128, D // QBLK], F32)
    xtm = p.sbuf("xtm", [128, 2, D], F32)
    xq = p.sbuf("xq", [128, 2, D], F32)
    xqT = p.sbuf("xqT", [128, KT, TOK], BF16)
    gch = p.sbuf("gch", [128, KT, 512], BF16)
    uch = p.sbuf("uch", [128, KT, 512], BF16)
    dch = p.sbuf("dch", [128, 4, D], BF16)
    gsc = p.sbuf("gsc", [128, 512], F32)
    hsc = p.sbuf("hsc", [128, 512], F32)
    hq = p.sbuf("hq", [128, 512], F32)
    hqT = p.sbuf("hqT", [128, 4, TOK], BF16)

    emit_ffn(p, dict(
        xT=x3T, xtm=xtm, xq=xq, xqT=xqT, gch=gch, uch=uch, dch=dch,
        gsc=gsc, hsc=hsc, hq=hq, hqT=hqT, stat=stat, sqs=sqs,
        qscr=qscr, qscrP=qscrP, qscrQ=qscrQ, qscrR=qscrR, qm=qm, qr=qr, qs=qs, cst=cst,
        identb_s=None, identf_s=identf_s,
        pb=pb, gq_d=gate_qt_in, uq_d=up_qt_in, dq_d=down_qt_in, xqT_dbg=None,
        ready_ve=X3_VE, ready_dx=p.val("dx")))

    OUT = p.dmas("gpsimd", [
        lambda g: g.dma_start(out=x4T_out.rearrange("(kt p) t -> p kt t", p=128), in_=x3T[:, :, :]),
    ], waits=[("ve", p.val("ve")), ("se", p.val("se")), ("pe", p.val("pe")),
              ("dx", p.val("dx")), ("dc", p.val("dc"))], sem="out")
    p.step("gpsimd", lambda g: None, waits=[("out", OUT)])
    return p


_L3 = None


def _get_l3():
    global _L3
    if _L3 is None:
        _L3 = build_L3().emit()
    return _L3


def _host_prep_l3(inputs, l2_results, x2_full, ffnw):
    A_log = np.asarray(inputs["L0_attn_b_A_log"])
    a_in = np.ascontiguousarray(A_log.reshape(DIT, 128, NST).transpose(1, 0, 2).reshape(128, DIT * NST))
    d_col = np.ascontiguousarray(np.asarray(inputs["L0_attn_b_D"]).reshape(DIT, 128).T)
    w_out_t = np.ascontiguousarray(np.asarray(inputs["L0_attn_b_W_out"]).T)
    identf = np.eye(128, dtype=np.float32)
    zS = np.zeros((DI, NST), np.float32)
    gq, uq, dq = ffnw

    in_maps = []
    for c in range(8):
        b, q = c // 4, c % 4
        r = l2_results
        def SP(qq, which):
            if qq < 0:
                return zS
            return np.asarray(r[b * 4 + qq][which])
        in_maps.append(dict(
            delta_in=np.asarray(r[c]["delta_o"]), xs_in=np.asarray(r[c]["xs_o"]),
            sres_in=np.asarray(r[c]["sres_o"]), bc_in=np.asarray(r[c]["bc_o"]),
            x2T_in=np.ascontiguousarray(x2_full[b, q * TOK:(q + 1) * TOK].T),
            x2tm_in=np.ascontiguousarray(x2_full[b, q * TOK:(q + 1) * TOK]),
            s1_in=SP(q - 1, "ssum_o"), p1_in=SP(q - 1, "psum_o"),
            s2_in=SP(q - 2, "ssum_o"), p2_in=SP(q - 2, "psum_o"),
            s3_in=SP(q - 3, "ssum_o"),
            a_log_in=a_in, d_in=d_col, w_out_t_in=w_out_t,
            gate_qt_in=gq, up_qt_in=uq, down_qt_in=dq, identf_in=identf,
        ))
    return in_maps


def _run_l3(inputs, l2_results, x2_full, ffnw):
    nc = _get_l3()
    return run_bass_kernel_spmd(nc, _host_prep_l3(inputs, l2_results, x2_full, ffnw),
                                core_ids=list(range(8)))


TOKP = 2176
NTPQ = TOKP // 128
TCS = [(0, 512), (512, 512), (1024, 512), (1536, 512), (2048, 128)]
TPBS = [0x2000000000, 0x3000000000, 0x6000000000, 0x7000000000,
        0x802000000000, 0x803000000000, 0x806000000000, 0x807000000000]
PNCMAP = [4, 5, 6, 7, 2, 3, 0, 1]
SEM_INC_OFF = 0x802701800
SBUF_PART_STRIDE = 0x40000
MB_OFF = 0x35800


def build_MEGA():
    import struct as _struct
    p = Prog()
    nc = p.nc

    x0_rot = nc.dram_tensor("x0_rot", [T, D], F32, kind="ExternalInput").ap()
    x0T_own = nc.dram_tensor("x0T_own", [D, TOKP], F32, kind="ExternalInput").ap()
    wq_t = nc.dram_tensor("wq_t", [D, D], F32, kind="ExternalInput").ap()
    wk_t = nc.dram_tensor("wk_t", [D, D], F32, kind="ExternalInput").ap()
    wv_t = nc.dram_tensor("wv_t", [D, D], F32, kind="ExternalInput").ap()
    wu_t = nc.dram_tensor("wu_t", [D, D], F32, kind="ExternalInput").ap()
    bu_in = nc.dram_tensor("bu_in", [128, KT], F32, kind="ExternalInput").ap()
    w_gate = nc.dram_tensor("w_gate", [IF, D], F32, kind="ExternalInput").ap()
    w_up = nc.dram_tensor("w_up", [IF, D], F32, kind="ExternalInput").ap()
    w_down = nc.dram_tensor("w_down", [D, IF], F32, kind="ExternalInput").ap()
    identb = nc.dram_tensor("identb", [128, 128], BF16, kind="ExternalInput").ap()
    w_in_t = nc.dram_tensor("w_in_t", [D, 2 * DI], F32, kind="ExternalInput").ap()
    wxp_t = nc.dram_tensor("wxp_t", [DI, 96], F32, kind="ExternalInput").ap()
    w_dt_t = nc.dram_tensor("w_dt_t", [DTR, DI], F32, kind="ExternalInput").ap()
    b_dt_in = nc.dram_tensor("b_dt_in", [128, DIT], F32, kind="ExternalInput").ap()
    a_log_in = nc.dram_tensor("a_log_in", [128, DIT * NST], F32, kind="ExternalInput").ap()
    conv_w_in = nc.dram_tensor("conv_w_in", [128, DIT * DCONV], F32, kind="ExternalInput").ap()
    conv_b_in = nc.dram_tensor("conv_b_in", [128, DIT], F32, kind="ExternalInput").ap()
    nw_rep_in = nc.dram_tensor("nw_rep_in", [128, D], F32, kind="ExternalInput").ap()
    d_in = nc.dram_tensor("d_in", [128, DIT], F32, kind="ExternalInput").ap()
    w_out_t_in = nc.dram_tensor("w_out_t_in", [DI, D], F32, kind="ExternalInput").ap()
    hm_in = nc.dram_tensor("hm_in", [128, 1], F32, kind="ExternalInput").ap()
    idin = nc.dram_tensor("idin", [1, 2], mybir.dt.uint32, kind="ExternalInput").ap()

    x4T_out = nc.dram_tensor("x4T_out", [D, TOK], F32, kind="ExternalOutput").ap()

    kT_d = nc.dram_tensor("kT_d", [D, T], BF16).ap()
    v_d = nc.dram_tensor("v_d", [T, H * 65], BF16).ap()
    gq_d = nc.dram_tensor("gq_d", [D, IF], BF16).ap()
    uq_d = nc.dram_tensor("uq_d", [D, IF], BF16).ap()
    dq_d = nc.dram_tensor("dq_d", [IF, D], BF16).ap()
    x2T_d = nc.dram_tensor("x2T_d", [D, TOKP], F32).ap()
    x2tm_d = nc.dram_tensor("x2tm_d", [TOKP, D], F32).ap()
    delta_d = nc.dram_tensor("delta_d", [DI, TOK], F32).ap()
    xs_d = nc.dram_tensor("xs_d", [DI, TOK], F32).ap()
    sres_d = nc.dram_tensor("sres_d", [DI, TOK], F32).ap()
    bc_d = nc.dram_tensor("bc_d", [32, TOK], F32).ap()
    bc_db = nc.dram_tensor("bc_db", [32, TOK], BF16).ap()
    psum_d = nc.dram_tensor("psum_d", [128, DIT, NST], F32).ap()
    ssum_d = nc.dram_tensor("ssum_d", [128, DIT, NST], F32).ap()
    yloc_d = nc.dram_tensor("yloc_d", [DI, TOK], F32).ap()
    yb_d = nc.dram_tensor("yb_d", [DI, TOK], BF16).ap()
    shS_e = nc.dram_tensor("shS_e", [128, DIT * NST], F32, addr_space="Shared").ap()
    shP_e = nc.dram_tensor("shP_e", [128, DIT * NST], F32, addr_space="Shared").ap()
    shS_o = nc.dram_tensor("shS_o", [128, DIT * NST], F32, addr_space="Shared").ap()
    shP_o = nc.dram_tensor("shP_o", [128, DIT * NST], F32, addr_space="Shared").ap()
    shH = nc.dram_tensor("shH", [128, DIT * NST], F32, addr_space="Shared").ap()

    _arena_t = p._arena()
    _arena_base = nc.lookup_mloc(_arena_t).addr
    mb_off = (_arena_base + Prog.ARENA_BYTES + 511) // 512 * 512
    assert mb_off + 1024 <= nc.SBUF_PARTITION_SIZE_BYTES, (mb_off, nc.SBUF_PARTITION_SIZE_BYTES)
    mbx = nc.alloc_sbuf_tensor_at("mbx", (128, 256), BF16, offset=mb_off)

    sem_spr = p.sem("spr")
    sem_mb = p.sem("mbx")
    sem_h6 = p.sem("h6r")
    sem_hr = p.sem("hr")
    sem_sbr = p.sem("sbr")

    identb_s = p.sbuf("identb_s", [128, 128], BF16)
    identf_s = p.sbuf("identf_s", [128, 128], F32)
    ones32 = p.sbuf("ones32", [128, 32], BF16)
    patt = p.sbuf("patt", [64, 128], F32)
    bu_s = p.sbuf("bu_s", [128, KT], F32)
    cst = p.sbuf("cst", [128, 2], F32)
    stat = p.sbuf("stat", [128, 8], F32)
    sqs = p.sbuf("sqs", [128, D], F32)
    idt_s = p.sbuf("idt_s", [1, 2], mybir.dt.uint32)
    x1T = p.sbuf("x1T", [128, KT, TOKP], F32)
    qT_s = p.sbuf("qT_s", [128, KT, TOKP], BF16)

    wqin = p.sbuf("wqin", [128, 1, 512], F32)
    wqout = p.sbuf("wqout", [128, 1, IF], BF16)
    wqtt = p.sbuf("wqtt", [128, 1, ITI, 128], BF16)
    wqscr = p.sbuf("wqscr", [128, 512], F32)
    wqscrP = p.sbuf("wqscrP", [128, 512], F32)
    wqscrQ = p.sbuf("wqscrQ", [128, 512], F32)
    wqscrR = p.sbuf("wqscrR", [128, 512], F32)
    wqm = p.sbuf("wqm", [128, 32], F32)
    wqr = p.sbuf("wqr", [128, 32], F32)
    wqs = p.sbuf("wqs", [128, 32], F32)

    wq_s = p.sbuf("wq_s", [128, KT, D], BF16)
    wk_s = p.sbuf("wk_s", [128, KT, D], BF16)
    wv_s = p.sbuf("wv_s", [128, KT, D], BF16)
    x0t = p.sbuf("x0t", [128, 2, D], F32)
    xnf = p.sbuf("xnf", [128, 2, D], F32)
    xnT = p.sbuf("xnT", [128, KT, 128], BF16)
    kvev = p.sbuf("kvev", [128, 2, 512], BF16)
    vtile = p.sbuf("vtile", [128, 2, H * 65], BF16)

    ps_y0 = p.psum("ps_y0", [128, 512])
    ps_y1 = p.psum("ps_y1", [128, 512])
    ps_row0 = p.psum("ps_row0", [64, 512])
    ps_row1 = p.psum("ps_row1", [64, 512])
    pb = [p.psum(f"pb{i}", [128, 512]) for i in range(4)]

    DX_W = p.dmas("gpsimd", [
        lambda g: g.dma_start(out=wq_s[:, :, :], in_=wq_t.rearrange("(kt p) o -> p kt o", p=128)),
        lambda g: g.dma_start(out=wk_s[:, :, :], in_=wk_t.rearrange("(kt p) o -> p kt o", p=128)),
        lambda g: g.dma_start(out=wv_s[:, :, :], in_=wv_t.rearrange("(kt p) o -> p kt o", p=128)),
        lambda g: g.dma_start(out=bu_s[:, :], in_=bu_in),
    ])
    DW_ID = p.dmas("sync", [
        lambda sy: sy.dma_start(out=identb_s[:, :], in_=identb),
        lambda sy: sy.dma_start(out=idt_s[:, :], in_=idin),
    ], sem="dw")

    def b(v):
        v.memset(ones32[:, :], 0.0)
        v.memset(ones32[:, 0:1], 1.0)
        v.memset(patt[:, :], 0.0)
        v.memset(patt[0:1, 0:64], 1.0)
        v.memset(patt[32:33, 64:128], 1.0)
        v.memset(cst[:, 0:1], EPS)
        v.memset(cst[:, 1:2], 0.0)
        v.memset(mbx.ap()[:, :], 0.0)
        return v.tensor_copy(identf_s[:, :], identb_s[:, :])
    p.step("vector", b, waits=[("dw", DW_ID)], inc="ve")

    _wq_jobs = []
    for srcd, dstd, ntl, Fw in ((w_gate, gq_d, ITI, D), (w_up, uq_d, ITI, D),
                                (w_down, dq_d, KT, IF)):
        for gI in range(ntl):
            _wq_jobs.append((srcd, dstd, gI, Fw))
    _wq_state = dict(i=0, qprev={}, outdma={}, trprev=0)

    def wq_emit_tile():
        st = _wq_state
        if st["i"] >= len(_wq_jobs):
            return
        srcd, dstd, gI, Fw = _wq_jobs[st["i"]]
        st["i"] += 1
        sl = st["i"] % 2
        chs = [(c * 512, 512) for c in range(Fw // 512)] + ([(Fw // 512 * 512, Fw % 512)]
                                                            if Fw % 512 else [])
        for (c0, cw) in chs:
            csl = (c0 // 512) % 2
            dxi = p.dmas("scalar", [
                lambda s, srcd=srcd, gI=gI, c0=c0, cw=cw: s.dma_start(
                    out=wqin[:, 0, 0:cw], in_=srcd[gI * 128:(gI + 1) * 128, c0:c0 + cw]),
            ], waits=[("ve", st["qprev"].get(0, 0)), ("ve", st["qprev"].get(1, 0))], sem="dq")
            st["qprev"][csl] = emit_quant(
                p, wqin[:, 0, 0:cw], wqout[:, 0, c0:c0 + cw],
                wqscr, wqscrP, wqscrQ, wqscrR, wqm, wqr, wqs, cw,
                waits=[("dq", dxi), ("dq", st.get("trd", 0))])
        A = Fw // 128
        trd = p.dmas("scalar", [
            lambda s, sl=sl, A=A, Fw=Fw: s.dma_start(
                out=wqtt[:, 0, 0:A, :], in_=wqout[:, 0, 0:Fw], transpose=True),
        ], waits=[("ve", st["qprev"][0]), ("ve", st["qprev"].get(1, 0)),
                  ("dq", st["outdma"].get(0, 0))], sem="dq")
        st["trd"] = trd
        st["outdma"][0] = p.dmas("scalar", [
            lambda s, dstd=dstd, gI=gI, A=A: s.dma_start(
                out=dstd.rearrange("(a p) r -> p a r", p=128)[:, 0:A, gI * 128:(gI + 1) * 128],
                in_=wqtt[:, 0, 0:A, :]),
        ], waits=[("dq", trd)], sem="dq")

    norm_done = {}
    ssq_done = {}
    kvev_dma = {}
    vdma_done = {}
    kv_pe = {}
    for i in range(NT):
        sl = i % 2
        w = [("ve", norm_done.get(i - 2, 0))]
        dx_in = p.dmas("gpsimd", [
            lambda g, i=i, sl=sl: g.dma_start(out=x0t[:, sl, :], in_=x0_rot[i * 128:(i + 1) * 128, :]),
        ], waits=w)
        vr = p.step("vector",
                    lambda v, sl=sl: v.reduce_sum(stat[:, 4 * sl:4 * sl + 1], x0t[:, sl, :], axis=AX),
                    waits=[("dx", dx_in)], inc="ve")
        p.step("vector",
               lambda v, sl=sl: v.tensor_single_scalar(stat[:, 4 * sl:4 * sl + 1],
                                                       stat[:, 4 * sl:4 * sl + 1], 1.0 / D, ALU.mult),
               waits=[("ve", vr)], inc="ve")
        w = [("dx", dx_in), ("ve", ssq_done.get(i - 1, 0))]
        se_sq = p.step("scalar",
                       lambda s, sl=sl: s.activation(out=sqs[:, :], in_=x0t[:, sl, :], func=AF.Square),
                       waits=w, inc="se")
        vr = p.step("vector",
                    lambda v, sl=sl: v.reduce_sum(stat[:, 4 * sl + 1:4 * sl + 2], sqs[:, :], axis=AX),
                    waits=[("se", se_sq)], inc="ve")
        def b(v, sl=sl):
            v.tensor_single_scalar(stat[:, 4 * sl + 1:4 * sl + 2],
                                   stat[:, 4 * sl + 1:4 * sl + 2], 1.0 / D, ALU.mult)
            return v.tensor_single_scalar(stat[:, 4 * sl + 2:4 * sl + 3], stat[:, 4 * sl:4 * sl + 1],
                                          stat[:, 4 * sl:4 * sl + 1], ALU.mult)
        p.step("vector", b, waits=[("ve", vr)], inc="ve")
        ssq_done[i] = p.step("vector",
                             lambda v, sl=sl: v.tensor_tensor(stat[:, 4 * sl + 1:4 * sl + 2],
                                                              stat[:, 4 * sl + 1:4 * sl + 2],
                                                              stat[:, 4 * sl + 2:4 * sl + 3],
                                                              ALU.subtract),
                             waits=(), inc="ve")
        se_std = p.step("scalar",
                        lambda s, sl=sl: s.activation(out=stat[:, 4 * sl + 3:4 * sl + 4],
                                                      in_=stat[:, 4 * sl + 1:4 * sl + 2],
                                                      func=AF.Sqrt, bias=cst[:, 0:1]),
                        waits=[("ve", ssq_done[i])], inc="se")
        p.step("vector",
               lambda v, sl=sl: v.reciprocal(stat[:, 4 * sl + 2:4 * sl + 3],
                                             stat[:, 4 * sl + 3:4 * sl + 4]),
               waits=[("se", se_std)], inc="ve")
        norm_done[i] = p.step("vector",
                              lambda v, sl=sl: v.tensor_scalar(xnf[:, sl, :], x0t[:, sl, :],
                                                               stat[:, 4 * sl:4 * sl + 1],
                                                               stat[:, 4 * sl + 2:4 * sl + 3],
                                                               ALU.subtract, ALU.mult),
                              waits=(), inc="ve")
        for half in range(2):
            w = [("ve", norm_done[i]), ("dw", DW_ID), ("se", p.val("se"))]
            if half == 0 and i >= 1:
                w.append(("pe", kv_pe[i - 1]))
            def b(tn, sl=sl, half=half):
                last = None
                for j in range(4):
                    jj = half * 4 + j
                    last = tn.transpose(pb[j][:, 0:128],
                                        xnf[:, sl, jj * 128:(jj + 1) * 128], identf_s[:, :])
                return last
            pe_tr = p.step("tensor", b, waits=w, inc="pe")
            def b(s, half=half):
                last = None
                for j in range(4):
                    last = s.copy(out=xnT[:, half * 4 + j, :], in_=pb[j][:, 0:128])
                return last
            p.step("scalar", b, waits=[("pe", pe_tr)], inc="se")
        trev = p.val("se")
        for og in range(2):
            w = [("se", trev), ("dx", DX_W), ("se", p.val("se"))]
            def b(tn, og=og):
                last = None
                for ot in range(4):
                    o = og * 4 + ot
                    for kt in range(KT):
                        last = tn.matmul(pb[ot][:, 0:128],
                                         wk_s[:, kt, o * 128:(o + 1) * 128],
                                         xnT[:, kt, :], start=(kt == 0), stop=(kt == KT - 1))
                return last
            pe_k = p.step("tensor", b, waits=w, inc="pe")
            w2 = [("pe", pe_k), ("dx", kvev_dma.get((i - 1, og), 0))]
            def b(s, og=og):
                last = None
                for ot in range(4):
                    last = s.copy(out=kvev[:, og, ot * 128:(ot + 1) * 128], in_=pb[ot][:, 0:128])
                return last
            se_ev = p.step("scalar", b, waits=w2, inc="se")
            kvev_dma[(i, og)] = p.dmas("gpsimd", [
                lambda g, og=og, i=i: g.dma_start(
                    out=kT_d.rearrange("(g p) t -> p g t", p=128)[:, og * 4:og * 4 + 4,
                                                                 i * 128:(i + 1) * 128],
                    in_=kvev[:, og, :].rearrange("p (a b) -> p a b", a=4)),
            ], waits=[("se", se_ev)])
        for oc in range(2):
            def b(tn, oc=oc):
                last = None
                for kt in range(KT):
                    last = tn.matmul(pb[oc][:, :], xnT[:, kt, :],
                                     wv_s[:, kt, oc * 512:(oc + 1) * 512],
                                     start=(kt == 0), stop=(kt == KT - 1))
                return last
            pe_v = p.step("tensor", b, waits=[("se", trev), ("se", p.val("se"))], inc="pe")
            w2 = [("pe", pe_v)]
            if oc == 0:
                w2.append(("dx", vdma_done.get(i - 2, 0)))
            p.step("scalar",
                   lambda s, oc=oc, sl=sl: s.copy(
                       out=vtile[:, sl, :].rearrange("p (h c) -> p h c", c=65)[:, oc * 8:oc * 8 + 8, 0:64],
                       in_=pb[oc][:, :].rearrange("p (h c) -> p h c", c=64)),
                   waits=w2, inc="se")
        se_vev = p.val("se")
        ve_ones = p.step("vector",
                         lambda v, sl=sl: v.memset(
                             vtile[:, sl, :].rearrange("p (h c) -> p h c", c=65)[:, :, 64:65], 1.0),
                         waits=[("se", se_vev)], inc="ve")
        vdma_done[i] = p.dmas("gpsimd", [
            lambda g, i=i, sl=sl: g.dma_start(out=v_d[i * 128:(i + 1) * 128, :], in_=vtile[:, sl, :]),
        ], waits=[("ve", ve_ones)])
        if i < NTPQ:
            for og in range(2):
                def b(tn, og=og):
                    last = None
                    for ot in range(4):
                        o = og * 4 + ot
                        for kt in range(KT):
                            last = tn.matmul(pb[ot][:, 0:128],
                                             wq_s[:, kt, o * 128:(o + 1) * 128],
                                             xnT[:, kt, :], start=(kt == 0), stop=(kt == KT - 1))
                    return last
                pe_q = p.step("tensor", b, waits=[("se", trev), ("se", p.val("se"))], inc="pe")
                def b(s, og=og, i=i):
                    last = None
                    for ot in range(4):
                        last = s.copy(out=qT_s[:, og * 4 + ot, i * 128:(i + 1) * 128],
                                      in_=pb[ot][:, 0:128])
                    return last
                p.step("scalar", b, waits=[("pe", pe_q)], inc="se")
        kv_pe[i] = p.val("pe")
        wq_emit_tile()

    while _wq_state["i"] < len(_wq_jobs):
        wq_emit_tile()
    FQ_DX = p.val("dq")
    KV_ALL_DX = p.val("dx")
    KV_ALL_SE = p.val("se")
    p.barrier()
    p.release("vtile", "kvev", "xnT", "xnf", "x0t", "wv_s", "wk_s", "wq_s")
    p.release("wqs", "wqr", "wqm", "wqscrR", "wqscrQ", "wqscrP", "wqscr",
              "wqtt", "wqout", "wqin")

    ynb = p.sbuf("ynb", [128, KT, TOKP], BF16)
    kTp = p.sbuf("kTp", [128, 1, T], BF16)
    vp = p.sbuf("vp", [128, 1, 64, 130], BF16)
    attT = p.sbuf("attT", [128, 2, 2, 512], BF16)
    yhp = p.sbuf("yhp", [128, TOKP], F32)
    rowt = p.sbuf("rowt", [64, TOKP], F32)
    p.step("vector", lambda v: v.memset(rowt[0:64, :], 1.0), waits=[], inc="ve")

    p.psum_release("pb3", "pb2", "pb1", "pb0")
    scb = [p.psum(f"scb{i}", [128, 1024]) for i in range(2)]
    hp_dma = {}
    hp_lastpe = {}
    scb_war_ve = 0
    ev_prev = 0
    for hp in range(H // 2):
        sl = 0
        w = [("dx", KV_ALL_DX), ("se", KV_ALL_SE)]
        if hp >= 1:
            w.append(("pe", hp_lastpe[hp - 1]))
        hp_dma[hp] = p.dmas("gpsimd", [
            lambda g, hp=hp, sl=sl: g.dma_start(out=kTp[:, sl, :], in_=kT_d[hp * 128:(hp + 1) * 128, :]),
            lambda g, hp=hp, sl=sl: g.dma_start(
                out=vp[:, sl, :, :],
                in_=v_d.rearrange("(sb p) c -> p sb c", p=128)[:, :, hp * 130:(hp + 1) * 130]),
        ], waits=w, sem="dk")
        for (tc0, tcw) in TCS:
            exp_hist = {}
            av_hist = {}
            sc_hist = {}

            def emit_sc(k, hp=hp, tc0=tc0, tcw=tcw, sl=sl):
                dsl = k % 2
                slot_free = exp_hist.get(k - 2, p.val("se") if k < 2 else 0)
                w = [("dk", hp_dma[hp]), ("ve", scb_war_ve), ("se", slot_free)]
                def b(tn, sl=sl, hp=hp, sb=k, dsl=dsl):
                    tn.matmul(scb[dsl][:, 0:tcw], kTp[0:64, sl, sb * 128:(sb + 1) * 128],
                              qT_s[0:64, hp, tc0:tc0 + tcw],
                              start=True, stop=True, tile_position=(0, 0))
                    return tn.matmul(scb[dsl][:, 512:512 + tcw], kTp[64:128, sl, sb * 128:(sb + 1) * 128],
                                     qT_s[64:128, hp, tc0:tc0 + tcw],
                                     start=True, stop=True, tile_position=(64, 0))
                return p.step("tensor", b, waits=w, inc="pe")

            sc_hist[0] = emit_sc(0)
            sc_hist[1] = emit_sc(1)
            for sb in range(64):
                dsl = sb % 2
                w = [("pe", sc_hist[sb]), ("pe", av_hist.get(sb - 2, 0))]
                if tcw == 512:
                    se_exp = p.step("scalar",
                                    lambda s, dsl=dsl: s.activation(
                                        out=attT[:, dsl, :, :].rearrange("p a b -> p (a b)"),
                                        in_=scb[dsl][:, :], func=AF.Exp, scale=ATT_SCALE),
                                    waits=w, inc="se")
                else:
                    def b(s, dsl=dsl, tcw=tcw):
                        s.activation(out=attT[:, dsl, 0, 0:tcw], in_=scb[dsl][:, 0:tcw],
                                     func=AF.Exp, scale=ATT_SCALE)
                        return s.activation(out=attT[:, dsl, 1, 0:tcw], in_=scb[dsl][:, 512:512 + tcw],
                                            func=AF.Exp, scale=ATT_SCALE)
                    se_exp = p.step("scalar", b, waits=w, inc="se")
                exp_hist[sb] = se_exp
                w = [("se", se_exp)]
                if sb == 0:
                    w.append(("ve", ev_prev))
                def b(tn, sl=sl, sb=sb, dsl=dsl, tcw=tcw):
                    st0 = (sb == 0)
                    sp = (sb == 63)
                    tn.matmul(ps_y0[0:64, 0:tcw], vp[:, sl, sb, 0:64], attT[:, dsl, 0, 0:tcw],
                              start=st0, stop=sp, tile_position=(0, 0))
                    tn.matmul(ps_y1[64:128, 0:tcw], vp[:, sl, sb, 65:129], attT[:, dsl, 1, 0:tcw],
                              start=st0, stop=sp, tile_position=(0, 64))
                    tn.matmul(ps_row0[0:32, 0:tcw], ones32[:, :], attT[:, dsl, 0, 0:tcw],
                              start=st0, stop=sp, tile_position=(0, 0))
                    return tn.matmul(ps_row1[32:64, 0:tcw], ones32[:, :], attT[:, dsl, 1, 0:tcw],
                                     start=st0, stop=sp, tile_position=(0, 32))
                av_hist[sb] = p.step("tensor", b, waits=w, inc="pe")
                if sb + 2 < 64:
                    sc_hist[sb + 2] = emit_sc(sb + 2)

            def b(v, tc0=tc0, tcw=tcw):
                v.tensor_copy(yhp[0:64, tc0:tc0 + tcw], ps_y0[0:64, 0:tcw])
                v.tensor_copy(yhp[64:128, tc0:tc0 + tcw], ps_y1[64:128, 0:tcw])
                v.tensor_copy(rowt[0:1, tc0:tc0 + tcw], ps_row0[0:1, 0:tcw])
                return v.tensor_copy(rowt[32:33, tc0:tc0 + tcw], ps_row1[32:33, 0:tcw])
            ev_prev = p.step("vector", b, waits=[("pe", p.val("pe"))], inc="ve")

        ve_rec = p.step("vector", lambda v: v.reciprocal(rowt[0:64, :], rowt[0:64, :]),
                        waits=[("ve", ev_prev)], inc="ve")
        w = [("ve", ve_rec), ("se", p.val("se"))]
        def b(tn):
            last = None
            for i5, (tc0, tcw) in enumerate(TCS):
                last = tn.matmul(scb[i5 % 2][:, (i5 // 2 % 2) * 512:(i5 // 2 % 2) * 512 + tcw],
                                 patt[:, :], rowt[0:64, tc0:tc0 + tcw],
                                 start=True, stop=True)
            return last
        pe_bc = p.step("tensor", b, waits=w, inc="pe")
        def b(v, hp=hp):
            last = None
            for i5, (tc0, tcw) in enumerate(TCS):
                last = v.tensor_tensor(ynb[:, hp, tc0:tc0 + tcw],
                                       yhp[:, tc0:tc0 + tcw],
                                       scb[i5 % 2][:, (i5 // 2 % 2) * 512:(i5 // 2 % 2) * 512 + tcw],
                                       ALU.mult)
            return last
        scb_war_ve = p.step("vector", b, waits=[("pe", pe_bc)], inc="ve")
        hp_lastpe[hp] = p.val("pe")
    p.psum_release("scb1", "scb0")
    pb = [p.psum(f"pbo{i}", [128, 512]) for i in range(4)]

    p.barrier()
    p.release("rowt", "yhp", "attT", "vp", "kTp")
    wu2_s = p.sbuf("wu2_s", [128, KT, D], BF16)

    dx_x0T = p.dmas("gpsimd", [
        lambda g: g.dma_start(out=x1T[:, :, :], in_=x0T_own.rearrange("(kt p) t -> p kt t", p=128)),
        lambda g: g.dma_start(out=wu2_s[:, :, :], in_=wu_t.rearrange("(kt p) o -> p kt o", p=128)),
    ])
    ATT_YNB_VE = p.val("ve")
    dn_free = {}
    for ot in range(KT):
        for i5, (tc0, tcw) in enumerate(TCS):
            pbi = (ot * 5 + i5) % 4
            w = [("ve", ATT_YNB_VE), ("ve", dn_free.get(pbi, 0)), ("dx", dx_x0T)]
            def b(tn, ot=ot, tc0=tc0, tcw=tcw, pbi=pbi):
                last = None
                for hp in range(KT):
                    last = tn.matmul(pb[pbi][:, 0:tcw], wu2_s[:, hp, ot * 128:(ot + 1) * 128],
                                     ynb[:, hp, tc0:tc0 + tcw],
                                     start=(hp == 0), stop=(hp == KT - 1))
                return last
            pe_op = p.step("tensor", b, waits=w, inc="pe")
            dn_free[pbi] = p.step(
                "vector",
                lambda v, ot=ot, tc0=tc0, tcw=tcw, pbi=pbi: v.scalar_tensor_tensor(
                    x1T[:, ot, tc0:tc0 + tcw], pb[pbi][:, 0:tcw], bu_s[:, ot:ot + 1],
                    x1T[:, ot, tc0:tc0 + tcw], ALU.add, ALU.add),
                waits=[("pe", pe_op), ("dx", dx_x0T)], inc="ve")
    X1_VE = p.val("ve")
    p.barrier()
    p.release("wu2_s", "ynb", "qT_s")


    qscr = p.sbuf("qscrF", [128, D], F32)
    qscrP = p.sbuf("qscrPF", [128, D], F32)
    qscrQ = p.sbuf("qscrQF", [128, D], F32)
    qscrR = p.sbuf("qscrRF", [128, D], F32)
    qm = p.sbuf("qmF", [128, D // QBLK], F32)
    qr = p.sbuf("qrF", [128, D // QBLK], F32)
    qs = p.sbuf("qsF", [128, D // QBLK], F32)
    xtm = p.sbuf("xtm", [128, 2, D], F32)
    xq = p.sbuf("xq", [128, 2, D], F32)
    xqT = p.sbuf("xqT", [128, KT, TOKP], BF16)
    gch = p.sbuf("gch", [128, KT, 512], BF16)
    uch = p.sbuf("uch", [128, KT, 512], BF16)
    dch = p.sbuf("dch", [128, 4, D], BF16)
    gsc = p.sbuf("gsc", [128, 512], F32)
    hsc = p.sbuf("hsc", [128, 512], F32)
    hq = p.sbuf("hq", [128, 512], F32)
    hqT = p.sbuf("hqT", [128, 4, TOKP], BF16)

    emit_ffn(p, dict(
        xT=x1T, xtm=xtm, xq=xq, xqT=xqT, gch=gch, uch=uch, dch=dch,
        gsc=gsc, hsc=hsc, hq=hq, hqT=hqT, stat=stat, sqs=sqs,
        qscr=qscr, qscrP=qscrP, qscrQ=qscrQ, qscrR=qscrR, qm=qm, qr=qr, qs=qs, cst=cst,
        identb_s=identb_s, identf_s=identf_s,
        pb=pb, gq_d=gq_d, uq_d=uq_d, dq_d=dq_d, xqT_dbg=None,
        ready_ve=X1_VE, ready_dx=FQ_DX, ready_sem="dq", ntiles=NTPQ, tcs=TCS))

    X2_VE = p.val("ve")
    p.dmas("gpsimd", [
        lambda g: g.dma_start(out=x2T_d.rearrange("(kt p) t -> p kt t", p=128), in_=x1T[:, :, :]),
    ], waits=[("ve", X2_VE)], sem="dx")
    x2ev = p.sbuf("x2ev", [128, 2, D], F32)
    x2t_dma = {}
    for tt in range(NTPQ):
        sl = tt % 2
        for half in range(2):
            w = [("ve", X2_VE), ("se", p.val("se")), ("ve", p.val("ve"))]
            if half == 0:
                w.append(("dx", x2t_dma.get(tt - 2, 0)))
            def b(tn, tt=tt, half=half):
                last = None
                for j in range(4):
                    jj = half * 4 + j
                    last = tn.transpose(pb[j][:, 0:128],
                                        x1T[:, jj, tt * 128:(tt + 1) * 128], identf_s[:, :])
                return last
            pe_tr = p.step("tensor", b, waits=w, inc="pe")
            def b(s, sl=sl, half=half):
                last = None
                for j in range(4):
                    last = s.copy(out=x2ev[:, sl, half * 512 + j * 128:half * 512 + (j + 1) * 128],
                                  in_=pb[j][:, 0:128])
                return last
            p.step("scalar", b, waits=[("pe", pe_tr)], inc="se")
        x2t_dma[tt] = p.dmas("gpsimd", [
            lambda g, tt=tt, sl=sl: g.dma_start(out=x2tm_d[tt * 128:(tt + 1) * 128, :],
                                                in_=x2ev[:, sl, :]),
        ], waits=[("se", p.val("se"))])
    p.barrier()
    p.release("x2ev", "hqT", "hq", "hsc", "gsc", "dch", "uch", "gch", "xqT", "xq", "xtm",
              "qsF", "qrF", "qmF", "qscrRF", "qscrQF", "qscrPF", "qscrF")
    p.release("x1T")

    nw_rep = p.sbuf("nw_rep", [128, D], F32)
    b_dt_s = p.sbuf("b_dt_s", [128, DIT], F32)
    a_s = p.sbuf("a_s", [128, DIT * NST], F32)
    cw_s = p.sbuf("cw_s", [128, DIT * DCONV], F32)
    cb_s = p.sbuf("cb_s", [128, DIT], F32)
    d2 = p.sbuf("d2", [128, DIT], F32)
    hm_s = p.sbuf("hm_s", [128, 1], F32)
    xn2T = p.sbuf("xn2T", [128, KT, TPAD], BF16)
    dbc = p.sbuf("dbc", [96, TOK], F32)
    dbcb = p.sbuf("dbcb", [64, TOK], BF16)
    dbcb2 = p.sbuf("dbcb2", [32, TOK], BF16)

    DXW = p.dmas("gpsimd", [
        lambda g: g.dma_start(out=nw_rep[:, :], in_=nw_rep_in),
        lambda g: g.dma_start(out=b_dt_s[:, :], in_=b_dt_in),
        lambda g: g.dma_start(out=a_s[:, :], in_=a_log_in),
        lambda g: g.dma_start(out=cw_s[:, :], in_=conv_w_in),
        lambda g: g.dma_start(out=cb_s[:, :], in_=conv_b_in),
        lambda g: g.dma_start(out=d2[:, :], in_=d_in),
        lambda g: g.dma_start(out=hm_s[:, :], in_=hm_in),
    ])
    se_a = p.step("scalar", lambda s: s.activation(out=a_s[:, :], in_=a_s[:, :], func=AF.Exp),
                  waits=[("dx", DXW)], inc="se")
    VE_A = p.step("vector", lambda v: v.tensor_single_scalar(a_s[:, :], a_s[:, :], -1.0, ALU.mult),
                  waits=[("se", se_a)], inc="ve")

    x2t = p.sbuf("x2t", [128, 2, D], F32)
    xnf2 = p.sbuf("xnf2", [128, 2, D], F32)
    hstg = p.sbuf("hstg", [4, D], F32)
    ln_done = {}
    for tt in range(NTP):
        sl = tt % 2
        if tt == 0:
            ve_z = p.step("vector", lambda v, sl=sl: v.memset(x2t[:, sl, :], 0.0),
                          waits=[("dx", p.val("dx"))], inc="ve")
            dx_h = p.dmas("gpsimd", [
                lambda g: g.dma_start(out=hstg[0:3, :], in_=x2tm_d[2048:2051, :]),
            ], waits=[])
            ve_hm = p.step("vector",
                           lambda v: v.tensor_scalar(hstg[0:3, :], hstg[0:3, :],
                                                     hm_s[0:3, 0:1], cst[0:3, 1:2],
                                                     ALU.mult, ALU.add),
                           waits=[("dx", dx_h), ("dx", DXW)], inc="ve")
            dx_in = p.dmas("gpsimd", [
                lambda g, sl=sl: g.dma_start(out=x2t[125:128, sl, :], in_=hstg[0:3, :]),
            ], waits=[("ve", ve_z), ("ve", ve_hm)])
        else:
            dx_in = p.dmas("gpsimd", [
                lambda g, tt=tt, sl=sl: g.dma_start(out=x2t[:, sl, :],
                                                    in_=x2tm_d[(tt - 1) * 128:tt * 128, :]),
            ], waits=[("ve", ln_done.get(tt - 2, 0))])
        vr = p.step("vector",
                    lambda v, sl=sl: v.reduce_sum(stat[:, 0:1], x2t[:, sl, :], axis=AX),
                    waits=[("dx", dx_in), ("ve", p.val("ve"))], inc="ve")
        p.step("vector",
               lambda v: v.tensor_single_scalar(stat[:, 0:1], stat[:, 0:1], 1.0 / D, ALU.mult),
               waits=[("ve", vr)], inc="ve")
        se_sq = p.step("scalar",
                       lambda s, sl=sl: s.activation(out=sqs[:, :], in_=x2t[:, sl, :], func=AF.Square),
                       waits=[("dx", dx_in), ("ve", p.val("ve"))], inc="se")
        vr = p.step("vector", lambda v: v.reduce_sum(stat[:, 1:2], sqs[:, :], axis=AX),
                    waits=[("se", se_sq)], inc="ve")
        def b(v):
            v.tensor_single_scalar(stat[:, 1:2], stat[:, 1:2], 1.0 / D, ALU.mult)
            return v.tensor_single_scalar(stat[:, 2:3], stat[:, 0:1], stat[:, 0:1], ALU.mult)
        p.step("vector", b, waits=[("ve", vr)], inc="ve")
        ve_var = p.step("vector",
                        lambda v: v.tensor_tensor(stat[:, 1:2], stat[:, 1:2], stat[:, 2:3],
                                                  ALU.subtract),
                        waits=(), inc="ve")
        se_std = p.step("scalar",
                        lambda s: s.activation(out=stat[:, 3:4], in_=stat[:, 1:2], func=AF.Sqrt,
                                               bias=cst[:, 0:1]),
                        waits=[("ve", ve_var)], inc="se")
        p.step("vector", lambda v: v.reciprocal(stat[:, 2:3], stat[:, 3:4]),
               waits=[("se", se_std)], inc="ve")
        p.step("vector",
               lambda v: v.tensor_scalar(stat[:, 4:5], stat[:, 1:2], stat[:, 2:3],
                                         stat[:, 2:3], ALU.mult, ALU.mult),
               waits=(), inc="ve")
        se_std2 = p.step("scalar",
                         lambda s: s.activation(out=stat[:, 5:6], in_=stat[:, 4:5], func=AF.Sqrt,
                                                bias=cst[:, 0:1]),
                         waits=[("ve", p.val("ve"))], inc="se")
        p.step("vector", lambda v: v.reciprocal(stat[:, 6:7], stat[:, 5:6]),
               waits=[("se", se_std2)], inc="ve")
        p.step("vector",
               lambda v: v.tensor_tensor(stat[:, 6:7], stat[:, 6:7], stat[:, 2:3], ALU.mult),
               waits=(), inc="ve")
        ve_ln = p.step("vector",
                       lambda v, sl=sl: v.tensor_scalar(xnf2[:, sl, :], x2t[:, sl, :],
                                                        stat[:, 0:1], stat[:, 6:7],
                                                        ALU.subtract, ALU.mult),
                       waits=(), inc="ve")
        ln_done[tt] = p.step("vector",
                             lambda v, sl=sl: v.tensor_tensor(xnf2[:, sl, :], xnf2[:, sl, :],
                                                              nw_rep[:, :], ALU.mult),
                             waits=(), inc="ve")
        for half in range(2):
            w = [("ve", ln_done[tt]), ("se", p.val("se"))]
            for j in range(4):
                jj = half * 4 + j
                pe_tr = p.step("tensor",
                               lambda tn, sl=sl, jj=jj, j=j: tn.transpose(
                                   pb[j % 2][:, 0:128], xnf2[:, sl, jj * 128:(jj + 1) * 128],
                                   identf_s[:, :]),
                               waits=w + [("se", p.val("se"))], inc="pe")
                p.step("scalar",
                       lambda s, tt=tt, jj=jj, j=j: s.copy(out=xn2T[:, jj, tt * 128:(tt + 1) * 128],
                                                           in_=pb[j % 2][:, 0:128]),
                       waits=[("pe", pe_tr)], inc="se")
    LN_SE = p.val("se")

    w_in_s = p.sbuf("w_in_s", [128, KT, 2 * DI], BF16)
    wxp_s = p.sbuf("wxp_s", [128, DIT, 96], BF16)
    wdt_s = p.sbuf("wdt_s", [64, DI], BF16)
    xcf = p.sbuf("xcf", [128, TPAD], F32)
    cacc = p.sbuf("cacc", [128, TOK], F32)
    xst = p.sbuf("xst", [128, TOK], F32)
    xsb = p.sbuf("xsb", [128, TOK], BF16)
    srt = p.sbuf("srt", [128, TOK], F32)

    p.psum_release("pbo3", "pbo2", "pbo1", "pbo0")
    p.psum_release("ps_row1", "ps_row0", "ps_y1", "ps_y0")
    pb = [p.psum(f"pm{i}", [128, 512]) for i in range(2)]
    ps_x = [p.psum(f"ps_x{i}", [96, 512]) for i in range(4)]
    ps_d = [p.psum(f"ps_d{i}", [128, 512]) for i in range(2)]

    DXW2 = p.dmas("gpsimd", [
        lambda g: g.dma_start(out=w_in_s[:, :, :], in_=w_in_t.rearrange("(kt p) r -> p kt r", p=128)),
        lambda g: g.dma_start(out=wxp_s[:, :, :], in_=wxp_t.rearrange("(dt p) c -> p dt c", p=128)),
        lambda g: g.dma_start(out=wdt_s[:, :], in_=w_dt_t),
    ])
    CH = [(c * 512, 512) for c in range(4)] + [(2048, 128)]
    xs_dma = {}
    for dt in range(DIT):
        for ci, (c0, cw) in enumerate(CH):
            w = [("se", LN_SE), ("dx", DXW2), ("se", p.val("se")), ("ve", p.val("ve"))]
            def b(tn, dt=dt, c0=c0, cw=cw, ci=ci):
                last = None
                for kt in range(KT):
                    last = tn.matmul(pb[ci % 2][:, 0:cw], w_in_s[:, kt, dt * 128:(dt + 1) * 128],
                                     xn2T[:, kt, c0:c0 + cw], start=(kt == 0), stop=(kt == KT - 1))
                return last
            pe_m = p.step("tensor", b, waits=w, inc="pe")
            p.step("vector",
                   lambda v, c0=c0, cw=cw, ci=ci: v.tensor_copy(xcf[:, c0:c0 + cw],
                                                                pb[ci % 2][:, 0:cw]),
                   waits=[("pe", pe_m)], inc="ve")
        def b(v, dt=dt):
            last = v.tensor_single_scalar(cacc[:, :], xcf[:, 125:125 + TOK],
                                          cw_s[:, dt * 4:dt * 4 + 1], ALU.mult)
            for k in range(1, DCONV):
                last = v.scalar_tensor_tensor(cacc[:, :], xcf[:, 125 + k:125 + k + TOK],
                                              cw_s[:, dt * 4 + k:dt * 4 + k + 1], cacc[:, :],
                                              ALU.mult, ALU.add)
            return last
        ve_cv = p.step("vector", b, waits=(), inc="ve")
        se_xs = p.step("scalar",
                       lambda s, dt=dt: s.activation(out=xst[:, :], in_=cacc[:, :], func=AF.Silu,
                                                     bias=cb_s[:, dt:dt + 1]),
                       waits=[("ve", ve_cv), ("dx", xs_dma.get(dt - 1, 0))], inc="se")
        ve_xsb = p.step("vector", lambda v: v.tensor_copy(xsb[:, :], xst[:, :]),
                        waits=[("se", se_xs)], inc="ve")
        xs_dma[dt] = p.dmas("gpsimd", [
            lambda g, dt=dt: g.dma_start(out=xs_d[dt * 128:(dt + 1) * 128, :], in_=xst[:, :]),
        ], waits=[("se", se_xs), ("ve", ve_xsb)])
        def b(tn, dt=dt):
            last = None
            for tc in range(4):
                last = tn.matmul(ps_x[tc][0:96, :], wxp_s[:, dt, :],
                                 xsb[:, tc * 512:(tc + 1) * 512],
                                 start=(dt == 0), stop=(dt == DIT - 1))
            return last
        p.step("tensor", b, waits=[("ve", ve_xsb)], inc="pe")
        for tc in range(4):
            w = [("se", p.val("se"), ), ("ve", p.val("ve"))]
            def b(tn, dt=dt, tc=tc):
                last = None
                for kt in range(KT):
                    last = tn.matmul(ps_d[tc % 2][:, :], w_in_s[:, kt, DI + dt * 128:DI + (dt + 1) * 128],
                                     xn2T[:, kt, 128 + tc * 512:128 + (tc + 1) * 512],
                                     start=(kt == 0), stop=(kt == KT - 1))
                return last
            pe_r = p.step("tensor", b, waits=w, inc="pe")
            p.step("scalar",
                   lambda s, tc=tc: s.activation(out=srt[:, tc * 512:(tc + 1) * 512],
                                                 in_=ps_d[tc % 2][:, :], func=AF.Silu),
                   waits=[("pe", pe_r), ("dx", p.mem.get("_sres_dma", 0))], inc="se")
        p.mem["_sres_dma"] = p.dmas("gpsimd", [
            lambda g, dt=dt: g.dma_start(out=sres_d[dt * 128:(dt + 1) * 128, :], in_=srt[:, :]),
        ], waits=[("se", p.val("se"))])
    XP_PE = p.val("pe")

    def b(v):
        last = None
        for tc in range(4):
            last = v.tensor_copy(dbc[0:96, tc * 512:(tc + 1) * 512], ps_x[tc][0:96, :])
        return last
    ve_dbc = p.step("vector", b, waits=[("pe", XP_PE)], inc="ve")
    ve_dbcb = p.step("vector", lambda v: v.tensor_copy(dbcb[0:64, :], dbc[0:64, :]),
                     waits=[("ve", ve_dbc)], inc="ve")
    ve_bcb2 = p.step("vector", lambda v: v.tensor_copy(dbcb2[0:32, :], dbc[64:96, :]),
                     waits=[("ve", ve_dbc)], inc="ve")
    p.dmas("gpsimd", [
        lambda g: g.dma_start(out=bc_d, in_=dbc[64:96, :]),
        lambda g: g.dma_start(out=bc_db, in_=dbcb2[0:32, :]),
    ], waits=[("ve", ve_bcb2)])

    dl1 = p.sbuf("dl1", [128, TOK], F32)
    dl2 = p.sbuf("dl2", [128, 512], F32)
    td = p.sbuf("td", [128, 1], F32)
    pp = p.sbuf("pp", [128, NST], F32)
    for dt in range(DIT):
        for tc in range(4):
            w = [("ve", ve_dbcb), ("se", p.val("se")), ("ve", p.val("ve"))]
            def b(tn, dt=dt, tc=tc):
                return tn.matmul(ps_d[tc % 2][:, :], wdt_s[0:64, dt * 128:(dt + 1) * 128],
                                 dbcb[0:64, tc * 512:(tc + 1) * 512], start=True, stop=True)
            pe_d = p.step("tensor", b, waits=w, inc="pe")
            def b(s, dt=dt, tc=tc):
                return s.activation(out=dl2[:, :], in_=ps_d[tc % 2][:, :], func=AF.Abs,
                                    bias=b_dt_s[:, dt:dt + 1])
            se1 = p.step("scalar", b, waits=[("pe", pe_d), ("dx", p.mem.get("_dl_dma", 0))], inc="se")
            se2 = p.step("scalar",
                         lambda s: s.activation(out=dl2[:, :], in_=dl2[:, :], func=AF.Exp,
                                                scale=-1.0),
                         waits=(), inc="se")
            se3 = p.step("scalar",
                         lambda s: s.activation(out=dl2[:, :], in_=dl2[:, :], func=AF.Ln,
                                                bias=1.0),
                         waits=(), inc="se")
            se4 = p.step("scalar",
                         lambda s, dt=dt, tc=tc: s.activation(out=dl1[:, tc * 512:(tc + 1) * 512],
                                                              in_=ps_d[tc % 2][:, :], func=AF.Relu,
                                                              bias=b_dt_s[:, dt:dt + 1]),
                         waits=(), inc="se")
            p.step("vector",
                   lambda v, tc=tc: v.tensor_tensor(dl1[:, tc * 512:(tc + 1) * 512],
                                                    dl1[:, tc * 512:(tc + 1) * 512],
                                                    dl2[:, :], ALU.add),
                   waits=[("se", se4), ("dx", p.mem.get("_dl_dma", 0))], inc="ve")
        vr = p.step("vector", lambda v: v.reduce_sum(td[:, :], dl1[:, :], axis=AX),
                    waits=[("dx", p.mem.get("_dl_dma", 0))], inc="ve")
        p.step("vector",
               lambda v, dt=dt: v.tensor_single_scalar(pp[:, :], a_s[:, dt * NST:(dt + 1) * NST],
                                                       td[:, 0:1], ALU.mult),
               waits=[("ve", vr), ("ve", VE_A)], inc="ve")
        se_p = p.step("scalar", lambda s: s.activation(out=pp[:, :], in_=pp[:, :], func=AF.Exp),
                      waits=[("ve", p.val("ve"))], inc="se")
        p.mem["_dl_dma"] = p.dmas("gpsimd", [
            lambda g, dt=dt: g.dma_start(out=delta_d[dt * 128:(dt + 1) * 128, :], in_=dl1[:, :]),
            lambda g, dt=dt: g.dma_start(out=psum_d[:, dt, :], in_=pp[:, :]),
        ], waits=[("se", se_p), ("ve", p.val("ve"))])

    p.barrier()
    p.release("pp", "td", "dl2", "dl1", "srt", "xsb", "xst", "cacc", "xcf",
              "wdt_s", "wxp_s", "w_in_s")
    p.psum_release("ps_d1", "ps_d0", "ps_x3", "ps_x2", "ps_x1", "ps_x0", "pm1", "pm0")

    def post_loc(p, dt, env):
        yac = env["yac"]
        p.mem["_yac_war"] = p.dmas("gpsimd", [
            lambda g, dt=dt: g.dma_start(out=yloc_d[dt * 128:(dt + 1) * 128, :], in_=yac[:, :]),
        ], waits=[("ve", p.val("ve"))])

    PRE_DX = p.val("dx")
    scan_t = dict(identf=identf_s, a_s=a_s,
                  delta_d=delta_d, xs_d=xs_d, bc_d=bc_d,
                  hinit=None, ssum_o=ssum_d, post=post_loc, bc_bf=bc_db,
                  ssum_slice=lambda d: ssum_d[:, d, :], ssum_sem="ds")
    emit_scan(p, scan_t)
    SCAN_ALL_DX = p.val("dx")
    SCAN_DS = p.val("ds")
    p.barrier()
    p.release(*scan_t["_scan_names"])
    p.release("hstg", "xnf2", "x2t", "dbcb2", "dbcb", "dbc", "xn2T")

    spS = p.sbuf("spS", [128, 256], F32)
    spP = p.sbuf("spP", [128, 256], F32)
    SA = p.sbuf("SA", [128, 256], F32)
    PA = p.sbuf("PA", [128, 256], F32)
    SB = p.sbuf("SB", [128, 256], F32)
    GG = p.sbuf("GG", [128, 256], F32)
    Gb = p.sbuf("Gb", [128, 256], BF16)
    mbf32 = p.sbuf("mbf32", [128, 256], F32)

    DX_SP = p.dmas("gpsimd", [
        lambda g: g.dma_start(out=spS[:, :], in_=ssum_d.rearrange("p a n -> p (a n)")),
        lambda g: g.dma_start(out=spP[:, :], in_=psum_d.rearrange("p a n -> p (a n)")),
    ], waits=[("ds", SCAN_DS), ("dx", PRE_DX)])

    sems = {k: p.sem(k) for k in ("spr", "mbx", "h6r", "hr", "sbr", "shw", "sbw", "ghw")}
    ve_sem = p.sem("ve")
    VE_MBC = p.step("vector", lambda v: v.tensor_copy(mbf32[:, :], mbx.ap()[:, :]),
                    waits=[("hr", 16)], inc="ve")
    dxv = p.val("dx")

    def comm_gp(g, dxv=dxv):
        g.wait_ge(p.sem("dx"), dxv)
        r = g.alloc_register("cid")
        g.load(r, idt_s[0:1, 0:1])
        shSr, shPr, shSro, shPro, shHr = shS_e, shP_e, shS_o, shP_o, shH
        with g.If_eq(r, 255):
            g.dma_start(out=SA[:, :], in_=spS[:, :]).then_inc(sems["sbw"], 16)
            g.dma_start(out=PA[:, :], in_=spP[:, :]).then_inc(sems["sbw"], 16)
            g.dma_start(out=SB[:, :], in_=spS[:, :]).then_inc(sems["sbw"], 16)
            g.wait_ge(sems["sbw"], 48)
            g.sem_inc(sems["sbr"], 1)
            g.sem_inc(sems["hr"], 16)
        g.end_ifs()
        for L in range(8):
            q = L % 4
            with g.If_eq(r, L):
                dstS = shS_e if L % 2 == 0 else shS_o
                dstP = shP_e if L % 2 == 0 else shP_o
                g.dma_start(out=dstS, in_=spS[:, :]).then_inc(sems["shw"], 16)
                g.dma_start(out=dstP, in_=spP[:, :]).then_inc(sems["shw"], 16)
                g.wait_ge(sems["shw"], 32)
                g.write(TPBS[PNCMAP[L ^ 1]] + SEM_INC_OFF + 4 * sems["spr"].num,
                        _struct.pack("<I", 1))
                g.wait_ge(sems["spr"], 1)
                if q == 0:
                    g.dma_start(out=SA[:, :], in_=shSro).then_inc(sems["sbw"], 16)
                    g.dma_start(out=PA[:, :], in_=shPro).then_inc(sems["sbw"], 16)
                    g.dma_start(out=SB[:, :], in_=spS[:, :]).then_inc(sems["sbw"], 16)
                    g.wait_ge(sems["sbw"], 48)
                    g.sem_inc(sems["sbr"], 1)
                    g.sem_inc(sems["hr"], 16)
                elif q == 1:
                    g.dma_start(out=SA[:, :], in_=spS[:, :]).then_inc(sems["sbw"], 16)
                    g.dma_start(out=PA[:, :], in_=spP[:, :]).then_inc(sems["sbw"], 16)
                    g.dma_start(out=SB[:, :], in_=shSr).then_inc(sems["sbw"], 16)
                    g.dma_start(out=mbx.ap()[:, :], in_=shSr).then_inc(sems["sbw"], 16)
                    g.wait_ge(sems["sbw"], 64)
                    g.sem_inc(sems["sbr"], 1)
                    g.sem_inc(sems["hr"], 16)
                elif q == 2:
                    g.wait_ge(sems["mbx"], 8)
                    g.sem_inc(sems["hr"], 16)
                    g.wait_ge(p.sem("ve"), VE_MBC)
                    g.dma_start(out=SA[:, :], in_=spS[:, :]).then_inc(sems["sbw"], 16)
                    g.dma_start(out=PA[:, :], in_=spP[:, :]).then_inc(sems["sbw"], 16)
                    g.dma_start(out=SB[:, :], in_=mbf32[:, :]).then_inc(sems["sbw"], 16)
                    g.wait_ge(sems["sbw"], 48)
                    g.sem_inc(sems["sbr"], 1)
                else:
                    g.dma_start(out=SA[:, :], in_=spS[:, :]).then_inc(sems["sbw"], 16)
                    g.dma_start(out=PA[:, :], in_=spP[:, :]).then_inc(sems["sbw"], 16)
                    g.dma_start(out=SB[:, :], in_=spS[:, :]).then_inc(sems["sbw"], 16)
                    g.wait_ge(sems["sbw"], 48)
                    g.sem_inc(sems["sbr"], 1)
                    g.wait_ge(sems["h6r"], 1)
                    g.dma_start(out=mbx.ap()[:, :], in_=shHr).then_inc(sems["sbw"], 16)
                    g.wait_ge(sems["sbw"], 64)
                    g.sem_inc(sems["hr"], 16)
        g.end_ifs()
        g.free_register(r)
    p.plan["gpsimd"].append(comm_gp)

    def b(v):
        v.tensor_tensor(GG[:, :], PA[:, :], SB[:, :], ALU.mult)
        return v.tensor_tensor(GG[:, :], GG[:, :], SA[:, :], ALU.add)
    GCV = p.step("vector", b, waits=[("sbr", 1)], inc="ve")
    VE_GB = p.step("vector", lambda v: v.tensor_copy(Gb[:, :], GG[:, :]), waits=(), inc="ve")

    def comm_gp2(g):
        r = g.alloc_register("cid2")
        g.load(r, idt_s[0:1, 0:1])
        for L in (2, 6):
            with g.If_eq(r, L):
                g.wait_ge(p.sem("ve"), GCV)
                g.dma_start(out=shH, in_=GG[:, :]).then_inc(sems["ghw"], 16)
                g.wait_ge(sems["ghw"], 16)
                g.write(TPBS[PNCMAP[L + 1]] + SEM_INC_OFF + 4 * sems["h6r"].num,
                        _struct.pack("<I", 1))
        g.end_ifs()
        g.free_register(r)
    p.plan["gpsimd"].append(comm_gp2)

    GBu = Gb.bitcast(mybir.dt.uint32)
    mb_num = sems["mbx"].num

    def make_sender(eng_idx, eng_name):
        def send_fn(e):
            e.wait_ge(ve_sem, VE_GB)
            r = e.alloc_register("sid")
            e.load(r, idt_s[0:1, 0:1])
            for L in (0, 1, 4, 5):
                q = L % 4
                tgt = (L // 4) * 4 + 2
                wbase = 0 if q == 0 else 64
                tpb = TPBS[PNCMAP[tgt]]
                with e.If_eq(r, L):
                    with e.register64(f"sad{L}") as sad:
                        regs = [e.alloc_register(f"sv{L}_{k}") for k in range(16)]
                        for pi in range(32):
                            part = eng_idx * 32 + pi
                            addr = tpb + part * SBUF_PART_STRIDE + mb_off + wbase * 4
                            e.reg_mov(sad, (addr & 0xFFFFFFFF, addr >> 32))
                            for cb in range(4):
                                e.load(regs, GBu[part:part + 1,
                                                 wbase + cb * 16:wbase + cb * 16 + 16])
                                for k in range(16):
                                    e.store(sad, regs[k])
                                    e.reg_add(sad, sad, 4)
                        for k in range(16):
                            e.free_register(regs[k])
                    e.write(tpb + SEM_INC_OFF + 4 * mb_num, _struct.pack("<I", 2))
            e.end_ifs()
            e.free_register(r)
        return send_fn
    for ei, en in enumerate(("gpsimd", "sync", "vector", "scalar")):
        p.plan[en].append(make_sender(ei, en))

    onesT = p.sbuf("onesT", [128, TOK], F32)
    cdt = p.sbuf("cdt", [128, TOK], F32)
    dat2 = p.sbuf("dat2", [128, 2, TOK], F32)
    dlc = p.sbuf("dlc", [128, 2, TOK], F32)
    ylc = p.sbuf("ylc", [128, TOK], F32)
    xsc = p.sbuf("xsc", [128, 2, TOK], F32)
    src_ = p.sbuf("src_", [128, 2, TOK], F32)
    rowc = p.sbuf("rowc", [1, 2, TOK], F32)
    ones1c = p.sbuf("ones1c", [1, 128], F32)
    ybt = p.sbuf("ybt", [128, TOK], BF16)
    pbC = p.psum("pbC2", [128, 2048])

    ve_o = p.step("vector", lambda v: v.memset(onesT[:, :], 1.0), waits=(), inc="ve")
    ve_o1 = p.step("vector", lambda v: v.memset(ones1c[:, :], 1.0), waits=(), inc="ve")
    yb_dma = {}
    row_used = {}
    pe_C = 0
    ve_cm = max(ve_o, ve_o1)
    for dt in range(DIT):
        sl = dt % 2
        dx_t = p.dmas("scalar", [
            lambda s, dt=dt, sl=sl: s.dma_start(out=dlc[:, sl, :],
                                                in_=delta_d[dt * 128:(dt + 1) * 128, :]),
            lambda s, dt=dt, sl=sl: s.dma_start(out=xsc[:, sl, :],
                                                in_=xs_d[dt * 128:(dt + 1) * 128, :]),
            lambda s, dt=dt, sl=sl: s.dma_start(out=src_[:, sl, :],
                                                in_=sres_d[dt * 128:(dt + 1) * 128, :]),
        ], waits=[("ve", p.mem.get("_co_used", 0))], sem="dt2")
        dx_y = p.dmas("scalar", [
            lambda s, dt=dt: s.dma_start(out=ylc[:, :], in_=yloc_d[dt * 128:(dt + 1) * 128, :]),
        ], waits=[("dx", p.mem.get("_yb_war", 0)), ("ve", p.mem.get("_yl_used", 0))], sem="dt2")
        ve_cd = p.step("vector",
                       lambda v, sl=sl: v.tensor_tensor_scan(cdt[:, :], onesT[:, :], dlc[:, sl, :],
                                                             0.0, ALU.mult, ALU.add),
                       waits=[("dt2", dx_t), ("ve", ve_o),
                              ("se", p.mem.get("_cd_se", 0))], inc="ve")
        for n in range(NST):
            bsl = n % 2
            dsl = n % 2
            dx_r = p.dmas("gpsimd", [
                lambda g, n=n, bsl=bsl: g.dma_start(out=rowc[0:1, bsl, :],
                                                    in_=bc_d[16 + n:17 + n, :]),
            ], waits=[("pe", row_used.get(bsl, 0))], sem="dr2")
            def bC(tn, bsl=bsl):
                last = None
                for c in range(4):
                    last = tn.matmul(pbC[:, c * 512:(c + 1) * 512], ones1c[0:1, :],
                                     rowc[0:1, bsl, c * 512:(c + 1) * 512],
                                     start=True, stop=True)
                return last
            pe_C = p.step("tensor", bC, waits=[("dr2", dx_r), ("ve", ve_cm),
                                               ("ve", ve_o1)], inc="pe")
            row_used[bsl] = pe_C
            se_e = p.step("scalar",
                          lambda s, dt=dt, n=n, dsl=dsl: s.activation(
                              out=dat2[:, dsl, :], in_=cdt[:, :], func=AF.Exp,
                              scale=a_s[:, dt * NST + n:dt * NST + n + 1]),
                          waits=[("ve", ve_cd), ("se", p.val("se")),
                                 ("ve", p.mem.get(("_co_h", dsl), 0))], inc="se")
            p.mem["_cd_se"] = se_e
            ve_m = p.step("vector",
                          lambda v, dt=dt, n=n, dsl=dsl: v.tensor_scalar(
                              dat2[:, dsl, :], dat2[:, dsl, :],
                              mbf32[:, dt * NST + n:dt * NST + n + 1], cst[:, 1:2],
                              ALU.mult, ALU.add),
                          waits=[("se", se_e)], inc="ve")
            ve_cm = p.step("vector",
                           lambda v, dsl=dsl: v.tensor_tensor(dat2[:, dsl, :], dat2[:, dsl, :],
                                                              pbC[:, :], ALU.mult),
                           waits=[("pe", pe_C)], inc="ve")
            p.mem[("_co_h", dsl)] = ve_cm
            p.step("vector",
                   lambda v, dsl=dsl: v.tensor_tensor(ylc[:, :], ylc[:, :], dat2[:, dsl, :],
                                                      ALU.add),
                   waits=[("dt2", dx_y)] if n == 0 else (), inc="ve")
        p.step("vector",
               lambda v, dt=dt, sl=sl: v.scalar_tensor_tensor(ylc[:, :], xsc[:, sl, :],
                                                              d2[:, dt:dt + 1], ylc[:, :],
                                                              ALU.mult, ALU.add),
               waits=(), inc="ve")
        p.mem["_co_used"] = p.step("vector",
                                   lambda v, sl=sl: v.tensor_tensor(ylc[:, :], ylc[:, :],
                                                                    src_[:, sl, :], ALU.mult),
                                   waits=(), inc="ve")
        ve_yb = p.step("vector", lambda v: v.tensor_copy(ybt[:, :], ylc[:, :]),
                       waits=[("dx", yb_dma.get(dt - 1, 0))], inc="ve")
        p.mem["_yl_used"] = ve_yb
        yb_dma[dt] = p.dmas("gpsimd", [
            lambda g, dt=dt: g.dma_start(out=yb_d[dt * 128:(dt + 1) * 128, :], in_=ybt[:, :]),
        ], waits=[("ve", ve_yb)])
        p.mem["_yb_war"] = 0
    CORR_DX = p.val("dx")

    p.barrier()
    p.release("ybt", "ones1c", "rowc", "src_", "xsc", "ylc", "dlc", "dat2", "cdt", "onesT")
    p.release("mbf32", "Gb", "GG", "SB", "PA", "SA", "spP", "spS")
    p.psum_release("pbC2")
    pb = [p.psum(f"pt{i}", [128, 512]) for i in range(4)]

    x3T = p.sbuf("x3T", [128, KT, TOK], F32)
    w_out_s = p.sbuf("w_out_s", [128, DIT, D], BF16)
    ybtc = p.sbuf("ybtc", [128, DIT, 512], BF16)
    DXT = p.dmas("gpsimd", [
        lambda g: g.dma_start(out=x3T[:, :, :],
                              in_=x2T_d.rearrange("(kt p) t -> p kt t", p=128)[:, :, 0:TOK]),
        lambda g: g.dma_start(out=w_out_s[:, :, :],
                              in_=w_out_t_in.rearrange("(dt p) o -> p dt o", p=128)),
    ])
    dn_free = {}
    for tc in range(4):
        dx_y = p.dmas("gpsimd", [
            lambda g, tc=tc: g.dma_start(
                out=ybtc[:, :, :],
                in_=yb_d.rearrange("(dt p) t -> p dt t", p=128)[:, :, tc * 512:(tc + 1) * 512]),
        ], waits=[("pe", p.val("pe"))])
        for ot in range(KT):
            pbi = ot % 2
            w = [("dx", dx_y), ("dx", DXT), ("ve", dn_free.get(pbi, 0))]
            def b(tn, tc=tc, ot=ot, pbi=pbi):
                last = None
                for dt in range(DIT):
                    last = tn.matmul(pb[pbi][:, :], w_out_s[:, dt, ot * 128:(ot + 1) * 128],
                                     ybtc[:, dt, :], start=(dt == 0), stop=(dt == DIT - 1))
                return last
            pe_w = p.step("tensor", b, waits=w, inc="pe")
            dn_free[pbi] = p.step(
                "vector",
                lambda v, tc=tc, ot=ot, pbi=pbi: v.tensor_tensor(
                    x3T[:, ot, tc * 512:(tc + 1) * 512], x3T[:, ot, tc * 512:(tc + 1) * 512],
                    pb[pbi][:, :], ALU.add),
                waits=[("pe", pe_w)], inc="ve")

    p.barrier()
    p.release("ybtc", "w_out_s")
    x2tl = p.sbuf("x2tl", [128, 2, D], F32)
    lnf = p.sbuf("lnf", [128, 2, D], F32)
    ln_done = {}
    for tt in range(NTO):
        sl = tt % 2
        dx_in = p.dmas("gpsimd", [
            lambda g, tt=tt, sl=sl: g.dma_start(out=x2tl[:, sl, :],
                                                in_=x2tm_d[tt * 128:(tt + 1) * 128, :]),
        ], waits=[("ve", ln_done.get(tt - 2, 0))])
        vr = p.step("vector",
                    lambda v, sl=sl: v.reduce_sum(stat[:, 0:1], x2tl[:, sl, :], axis=AX),
                    waits=[("dx", dx_in)], inc="ve")
        p.step("vector",
               lambda v: v.tensor_single_scalar(stat[:, 0:1], stat[:, 0:1], 1.0 / D, ALU.mult),
               waits=[("ve", vr)], inc="ve")
        se_sq = p.step("scalar",
                       lambda s, sl=sl: s.activation(out=sqs[:, :], in_=x2tl[:, sl, :], func=AF.Square),
                       waits=[("dx", dx_in)], inc="se")
        vr = p.step("vector", lambda v: v.reduce_sum(stat[:, 1:2], sqs[:, :], axis=AX),
                    waits=[("se", se_sq)], inc="ve")
        def b(v):
            v.tensor_single_scalar(stat[:, 1:2], stat[:, 1:2], 1.0 / D, ALU.mult)
            return v.tensor_single_scalar(stat[:, 2:3], stat[:, 0:1], stat[:, 0:1], ALU.mult)
        p.step("vector", b, waits=[("ve", vr)], inc="ve")
        ve_var = p.step("vector",
                        lambda v: v.tensor_tensor(stat[:, 1:2], stat[:, 1:2], stat[:, 2:3],
                                                  ALU.subtract),
                        waits=(), inc="ve")
        se_std = p.step("scalar",
                        lambda s: s.activation(out=stat[:, 3:4], in_=stat[:, 1:2], func=AF.Sqrt,
                                               bias=cst[:, 0:1]),
                        waits=[("ve", ve_var)], inc="se")
        p.step("vector", lambda v: v.reciprocal(stat[:, 2:3], stat[:, 3:4]),
               waits=[("se", se_std)], inc="ve")
        ln_done[tt] = p.step("vector",
                             lambda v, sl=sl: v.tensor_scalar(lnf[:, sl, :], x2tl[:, sl, :],
                                                              stat[:, 0:1], stat[:, 2:3],
                                                              ALU.subtract, ALU.mult),
                             waits=(), inc="ve")
        for j in range(KT):
            pe_tr = p.step("tensor",
                           lambda tn, sl=sl, j=j: tn.transpose(pb[2 + j % 2][:, 0:128],
                                                               lnf[:, sl, j * 128:(j + 1) * 128],
                                                               identf_s[:, :]),
                           waits=[("ve", ln_done[tt]),
                                  ("ve", p.val("ve")), ("se", p.val("se"))], inc="pe")
            p.step("vector",
                   lambda v, tt=tt, j=j: v.tensor_tensor(x3T[:, j, tt * 128:(tt + 1) * 128],
                                                         x3T[:, j, tt * 128:(tt + 1) * 128],
                                                         pb[2 + j % 2][:, 0:128], ALU.add),
                   waits=[("pe", pe_tr)], inc="ve")
    X3_VE = p.val("ve")

    p.barrier()
    p.release("lnf", "x2tl")
    qscr = p.sbuf("qscrG", [128, D], F32)
    qscrP = p.sbuf("qscrPG", [128, D], F32)
    qscrQ = p.sbuf("qscrQG", [128, D], F32)
    qscrR = p.sbuf("qscrRG", [128, D], F32)
    qm = p.sbuf("qmG", [128, D // QBLK], F32)
    qr = p.sbuf("qrG", [128, D // QBLK], F32)
    qs = p.sbuf("qsG", [128, D // QBLK], F32)
    xtm = p.sbuf("xtmG", [128, 2, D], F32)
    xq = p.sbuf("xqG", [128, 2, D], F32)
    xqT = p.sbuf("xqTG", [128, KT, TOK], BF16)
    gch = p.sbuf("gchG", [128, KT, 512], BF16)
    uch = p.sbuf("uchG", [128, KT, 512], BF16)
    dch = p.sbuf("dchG", [128, 4, D], BF16)
    gsc = p.sbuf("gscG", [128, 512], F32)
    hsc = p.sbuf("hscG", [128, 512], F32)
    hq = p.sbuf("hqG", [128, 512], F32)
    hqT = p.sbuf("hqTG", [128, 4, TOK], BF16)

    emit_ffn(p, dict(
        xT=x3T, xtm=xtm, xq=xq, xqT=xqT, gch=gch, uch=uch, dch=dch,
        gsc=gsc, hsc=hsc, hq=hq, hqT=hqT, stat=stat, sqs=sqs,
        qscr=qscr, qscrP=qscrP, qscrQ=qscrQ, qscrR=qscrR, qm=qm, qr=qr, qs=qs, cst=cst,
        identb_s=identb_s, identf_s=identf_s,
        pb=pb, gq_d=gq_d, uq_d=uq_d, dq_d=dq_d, xqT_dbg=None,
        ready_ve=X3_VE, ready_dx=p.val("dx")))

    OUT = p.dmas("gpsimd", [
        lambda g: g.dma_start(out=x4T_out.rearrange("(kt p) t -> p kt t", p=128), in_=x3T[:, :, :]),
    ], waits=[("ve", p.val("ve")), ("se", p.val("se")), ("pe", p.val("pe")),
              ("dx", p.val("dx")), ("dc", p.val("dc")), ("dt2", p.val("dt2"))], sem="out")
    p.step("gpsimd", lambda g: None, waits=[("out", OUT)])
    return p


_MEGA = None


def _get_mega():
    global _MEGA
    if _MEGA is None:
        _MEGA = build_MEGA().emit()
    return _MEGA


def _host_prep_mega(inputs):
    x0 = np.asarray(inputs["L0_attn_a_x"])
    wq_t = np.ascontiguousarray(np.asarray(inputs["L0_attn_a_Wq"]).T)
    wk_t = np.ascontiguousarray(np.asarray(inputs["L0_attn_a_Wk"]).T)
    wv_t = np.ascontiguousarray(np.asarray(inputs["L0_attn_a_Wv"]).T)
    wu_t = np.ascontiguousarray(np.asarray(inputs["L0_attn_a_Wu"]).T)
    bu = np.ascontiguousarray(np.asarray(inputs["L0_attn_a_bu"]).reshape(KT, 128).T)
    w_gate = np.ascontiguousarray(np.asarray(inputs["L0_ffn_w_gate"]))
    w_up = np.ascontiguousarray(np.asarray(inputs["L0_ffn_w_up"]))
    w_down = np.ascontiguousarray(np.asarray(inputs["L0_ffn_w_down"]))
    identb = np.eye(128, dtype=NPBF16)
    W_in = np.asarray(inputs["L0_attn_b_W_in"])
    w_in_t = np.ascontiguousarray(W_in.T)
    wxp_t = np.ascontiguousarray(np.asarray(inputs["L0_attn_b_W_xproj"]).T)
    w_dt_t = np.ascontiguousarray(np.asarray(inputs["L0_attn_b_W_dt"]).T)
    b_dt = np.ascontiguousarray(np.asarray(inputs["L0_attn_b_b_dt"]).reshape(DIT, 128).T)
    A_log = np.asarray(inputs["L0_attn_b_A_log"])
    a_in = np.ascontiguousarray(A_log.reshape(DIT, 128, NST).transpose(1, 0, 2).reshape(128, DIT * NST))
    conv_w = np.ascontiguousarray(np.asarray(inputs["L0_attn_b_conv_w"]).reshape(DIT, 128, DCONV)
                                  .transpose(1, 0, 2).reshape(128, DIT * DCONV))
    conv_b = np.ascontiguousarray(np.asarray(inputs["L0_attn_b_conv_b"]).reshape(DIT, 128).T)
    nw_rep = np.ascontiguousarray(np.broadcast_to(np.asarray(inputs["L0_attn_b_norm_w"])[None, :],
                                                  (128, D)))
    d_col = np.ascontiguousarray(np.asarray(inputs["L0_attn_b_D"]).reshape(DIT, 128).T)
    w_out_t = np.ascontiguousarray(np.asarray(inputs["L0_attn_b_W_out"]).T)

    in_maps = []
    for c in range(8):
        b, q = c // 4, c % 4
        xb = x0[b]
        own = xb[q * TOK:(q + 1) * TOK]
        prev3 = xb[(q * TOK - 3) % T:(q * TOK - 3) % T + 3] if q > 0 else xb[T - 3:T]
        rest_idx = np.ones(T, dtype=bool)
        rest_idx[q * TOK:(q + 1) * TOK] = False
        pr0 = (q * TOK - 3) % T
        rest_idx[pr0:pr0 + 3] = False
        rest = xb[rest_idx]
        rot = np.concatenate([own, prev3, rest], axis=0)
        x0T = np.zeros((D, TOKP), np.float32)
        x0T[:, 0:TOK] = own.T
        x0T[:, TOK:TOK + 3] = prev3.T
        hm = np.full((128, 1), 0.0 if q == 0 else 1.0, np.float32)
        in_maps.append(dict(
            x0_rot=np.ascontiguousarray(rot), x0T_own=x0T,
            wq_t=wq_t, wk_t=wk_t, wv_t=wv_t, wu_t=wu_t, bu_in=bu,
            w_gate=w_gate, w_up=w_up, w_down=w_down, identb=identb,
            w_in_t=w_in_t, wxp_t=wxp_t, w_dt_t=w_dt_t, b_dt_in=b_dt,
            a_log_in=a_in, conv_w_in=conv_w, conv_b_in=conv_b, nw_rep_in=nw_rep,
            d_in=d_col, w_out_t_in=w_out_t, hm_in=hm,
            idin=np.array([[c, 0]], np.uint32),
        ))
    return in_maps


def kernel(**inputs):
    nc = _get_mega()
    res = run_bass_kernel_spmd(nc, _host_prep_mega(inputs), core_ids=list(range(8)))
    out = np.empty((2, 8192, D), np.float32)
    for c in range(8):
        b, q = c // 4, c % 4
        out[b, q * TOK:(q + 1) * TOK] = np.asarray(res.results[c]["x4T_out"]).T
    return out
